# version 04
# baseline (speedup 1.0000x reference)
import math
import sys
import numpy as np
import ml_dtypes
from contextlib import ExitStack

import concourse.bass as bass
import concourse.mybir as mybir
from concourse.bass_utils import run_bass_kernel_spmd

F32 = mybir.dt.float32
BF16 = mybir.dt.bfloat16
AF = mybir.ActivationFunctionType
ALU = mybir.AluOpType
AX = mybir.AxisListType

T = 4096
D = 1024
NDT = 8
NCHUNK = 8
CH = 512
NCORES = 8


class Ctx:
    ENGINES = ("sync", "vector", "scalar", "tensor", "gpsimd")

    def __init__(self, nc, stack):
        self.nc = nc
        self.stack = stack
        self.sem_ = {}
        self.cnt = {}
        self.dma_streams = set()
        for name in ("ph", "sv", "ss", "st", "sg", "qld", "qst"):
            self.getsem(name)

    def getsem(self, name):
        if name not in self.sem_:
            self.sem_[name] = self.stack.enter_context(self.nc.semaphore(name))
            self.cnt[name] = 0
        return self.sem_[name]

    @property
    def sem(self):
        return self

    def __getitem__(self, name):
        return self.getsem(name)

    def tick(self, name, amount=1):
        self.getsem(name)
        if amount >= 16:
            self.dma_streams.add(name)
        self.cnt[name] += amount
        return self.cnt[name]


def sbuf(ctx, name, shape, dt=F32):
    return ctx.stack.enter_context(ctx.nc.sbuf_tensor(name, shape, dt))


def psum(ctx, name, shape, dt=F32):
    return ctx.stack.enter_context(ctx.nc.psum_tensor(name, shape, dt))


class Phase:

    def __init__(self, ctx):
        self.ctx = ctx
        self._planned = False
        self.mstack = ExitStack()

    def sb(self, name, shape, dt=F32):
        return self.mstack.enter_context(self.ctx.nc.sbuf_tensor(name, shape, dt))

    def psb(self, name, shape, dt=F32):
        return self.mstack.enter_context(self.ctx.nc.psum_tensor(name, shape, dt))

    def alloc(self):
        pass

    def emit(self, eng, e):
        if not self._planned:
            self.alloc()
            self.plan()
            self._planned = True
        getattr(self, "em_" + eng)(e)

    def dma(self, e, out, in_, q):
        e.dma_start(out=out, in_=in_).then_inc(self.ctx.sem[q], 16)

    def inc(self, inst, name):
        inst.then_inc(self.ctx.sem[name], 1)

    def w(self, e, name, val):
        e.wait_ge(self.ctx.sem[name], val)

    def em_sync(self, e):
        pass

    def em_vector(self, e):
        pass

    def em_scalar(self, e):
        pass

    def em_tensor(self, e):
        pass

    def em_gpsimd(self, e):
        pass



class SPhase(Phase):

    def __init__(self, ctx, io=None):
        super().__init__(ctx)
        self.io = io
        self.items = []
        self.keys = {}
        self._built = False

    def op(self, eng, fn, inc=None, key=None):
        self.items.append(("op", eng, fn, inc, key))

    def sdma(self, eng, out, in_, q="qld", key=None):
        self.items.append(("dma", eng, out, in_, q, key))

    def wt(self, eng, key):
        self.items.append(("wait", eng, key))

    def wt_val(self, eng, sem, val):
        self.items.append(("waitv", eng, sem, val))

    def alias(self, key, srckey):
        self.items.append(("alias", key, srckey))

    def plan(self):
        if not self._built:
            self.build()
            self._built = True
        c = self.ctx
        for it in self.items:
            if it[0] == "dma":
                _, eng, out, in_, q, key = it
                v = c.tick(q, 16)
                if key is not None:
                    self.keys[key] = (q, v)
            elif it[0] == "op":
                _, eng, fn, incsem, key = it
                if incsem is not None:
                    v = c.tick(incsem)
                    if key is not None:
                        self.keys[key] = (incsem, v)
            elif it[0] == "alias":
                self.keys[it[1]] = self.keys[it[2]]

    def emit_engine(self, name, e):
        for it in self.items:
            if it[0] == "dma" and it[1] == name:
                _, eng, out, in_, q, key = it
                e.dma_start(out=out, in_=in_).then_inc(self.ctx.sem[q], 16)
            elif it[0] == "op" and it[1] == name:
                _, eng, fn, incsem, key = it
                inst = fn(e)
                if incsem is not None:
                    inst.then_inc(self.ctx.sem[incsem], 1)
            elif it[0] == "wait" and it[1] == name:
                sem, val = self.keys[it[2]]
                e.wait_ge(self.ctx.sem[sem], val)
            elif it[0] == "waitv" and it[1] == name:
                e.wait_ge(self.ctx.sem[it[2]], it[3])

    def em_sync(self, e):
        self.emit_engine("sync", e)

    def em_vector(self, e):
        self.emit_engine("vector", e)

    def em_scalar(self, e):
        self.emit_engine("scalar", e)

    def em_tensor(self, e):
        self.emit_engine("tensor", e)

    def em_gpsimd(self, e):
        self.emit_engine("gpsimd", e)


class P_LN(Phase):
    def __init__(self, ctx, src, dstf, dstb, tag, shift=False, gb=None,
                 res_src=None):
        super().__init__(ctx)
        self.src, self.dstf, self.dstb, self.tag = src, dstf, dstb, tag
        self.shift = shift
        self.gb = gb

    def alloc(self):
        ctx, tag = self.ctx, self.tag
        sbuf, psum = self.sb, self.psb
        self.xb = [sbuf(f"{tag}_xb{i}", [128, NDT * CH]) for i in range(2)]
        self.cb = sbuf(f"{tag}_cb", [128, NDT * CH], BF16)
        self.sq = sbuf(f"{tag}_sq", [128, NDT * CH], BF16)
        self.onesb = sbuf(f"{tag}_onesb", [128, 1], BF16)
        self.onesf = sbuf(f"{tag}_onesf", [1, 128])
        self.rows = sbuf(f"{tag}_rows", [1, 4 * CH])
        self.epscol = sbuf(f"{tag}_eps", [1, 1])
        self.xnf = [sbuf(f"{tag}_xnf{i}", [128, NDT * CH]) for i in range(2)]
        self.xnb = [sbuf(f"{tag}_xnb{i}", [128, NDT * CH], BF16) for i in range(2)]
        if self.gb is not None:
            self.gcols = sbuf(f"{tag}_g", [128, NDT])
            self.bcols = sbuf(f"{tag}_b", [128, NDT])
        self.ps_s = psum(f"{tag}_pss", [1, CH])
        self.ps_q = psum(f"{tag}_psq", [1, CH])
        self.ps_b = psum(f"{tag}_psb", [128, 2 * CH])

    def plan(self):
        c = self.ctx
        self.v_wld = c.cnt.get("Lw", 0)
        if self.gb is not None:
            self.v_wld = c.tick("Lw", 32)
        self.v_ld, self.v_cast, self.v_sq, self.v_stats = [], [], [], []
        self.v_rows, self.v_rstd, self.v_mr, self.v_bcast = [], [], [], []
        self.v_out, self.v_store = [], []
        for ch in range(NCHUNK):
            self.v_ld.append(c.tick(f"Lc{ch % 2}", 16 * NDT))
        for ch in range(NCHUNK):
            self.v_cast.append(c.tick("sv"))
            self.v_rows.append(c.tick("sv"))
            self.v_mr.append(c.tick("sv"))
            self.v_out.append(c.tick("sv"))
        self.v_ln = []
        for ch in range(NCHUNK):
            self.v_sq.append(c.tick("ss"))
            self.v_ln.append(c.tick("ss"))
            self.v_rstd.append(c.tick("ss"))
        for ch in range(NCHUNK):
            self.v_stats.append(c.tick("st"))
            self.v_bcast.append(c.tick("st"))
        nst = NDT * ((self.dstf is not None) + (self.dstb is not None))
        for ch in range(NCHUNK):
            self.v_store.append(c.tick(f"Ls{ch % 2}", 16 * nst))

    def em_sync(self, e):
        if self.gb is not None:
            self.dma(e, self.gcols[:], self.gb[0], "Lw")
            self.dma(e, self.bcols[:], self.gb[1], "Lw")
        for ch in range(NCHUNK):
            q = f"Lc{ch % 2}"
            if ch >= 2:
                self.w(e, "sv", self.v_out[ch - 2])
            for dt in range(NDT):
                dst = self.xb[ch % 2][:, dt * CH:(dt + 1) * CH]
                if self.shift and dt >= 4:
                    if ch == 0:
                        self.dma(e, dst[:, 1:CH],
                                 self.src[dt * 128:(dt + 1) * 128, 0:CH - 1], q)
                    else:
                        self.dma(e, dst,
                                 self.src[dt * 128:(dt + 1) * 128,
                                          ch * CH - 1:(ch + 1) * CH - 1], q)
                else:
                    self.dma(e, dst,
                             self.src[dt * 128:(dt + 1) * 128,
                                      ch * CH:(ch + 1) * CH], q)

    def em_vector(self, e):
        c = self.ctx
        e.memset(self.onesb[:], 1.0)
        e.memset(self.onesf[:], 1.0)
        e.memset(self.epscol[:], 1e-5)
        rows = self.rows
        mu, var = rows[:, 0:CH], rows[:, CH:2 * CH]
        rstd, mr = rows[:, 2 * CH:3 * CH], rows[:, 3 * CH:4 * CH]
        if self.gb is not None:
            self.w(e, "Lw", self.v_wld)
        for ch in range(NCHUNK):
            xb = self.xb[ch % 2]
            self.w(e, f"Lc{ch % 2}", self.v_ld[ch])
            if self.shift and ch == 0:
                for dt in range(4, NDT):
                    e.memset(xb[:, dt * CH:dt * CH + 1], 0.0)
            for dt in range(NDT):
                sl = slice(dt * CH, (dt + 1) * CH)
                i = e.tensor_copy(out=self.cb[:, sl], in_=xb[:, sl])
            self.inc(i, "sv")
            self.w(e, "st", self.v_stats[ch])
            e.tensor_scalar(out=mu, in0=self.ps_s[:], scalar1=1.0 / D,
                            scalar2=None, op0=ALU.mult)
            e.tensor_scalar(out=var, in0=self.ps_q[:], scalar1=1.0 / D,
                            scalar2=None, op0=ALU.mult)
            e.tensor_mul(out=mr, in0=mu, in1=mu)
            i = e.tensor_sub(out=var, in0=var, in1=mr)
            self.inc(i, "sv")
            self.w(e, "ss", self.v_rstd[ch])
            i = e.tensor_mul(out=mr, in0=mu, in1=rstd)
            self.inc(i, "sv")
            self.w(e, "st", self.v_bcast[ch])
            if ch >= 2:
                self.w(e, f"Ls{ch % 2}", self.v_store[ch - 2])
            xnf, xnb = self.xnf[ch % 2], self.xnb[ch % 2]
            for dt in range(NDT):
                sl = slice(dt * CH, (dt + 1) * CH)
                e.tensor_mul(out=xnf[:, sl], in0=xb[:, sl], in1=self.ps_b[:, 0:CH])
                e.tensor_sub(out=xnf[:, sl], in0=xnf[:, sl],
                             in1=self.ps_b[:, CH:2 * CH])
                if self.gb is not None:
                    e.tensor_scalar(out=xnf[:, sl], in0=xnf[:, sl],
                                    scalar1=self.gcols[:, dt:dt + 1],
                                    scalar2=self.bcols[:, dt:dt + 1],
                                    op0=ALU.mult, op1=ALU.add)
                i = e.tensor_copy(out=xnb[:, sl], in_=xnf[:, sl])
            self.inc(i, "sv")

    def em_scalar(self, e):
        rows = self.rows
        var, rstd = rows[:, CH:2 * CH], rows[:, 2 * CH:3 * CH]
        for ch in range(NCHUNK):
            xb = self.xb[ch % 2]
            self.w(e, f"Lc{ch % 2}", self.v_ld[ch])
            for dt in range(NDT):
                sl = slice(dt * CH, (dt + 1) * CH)
                i = e.activation(out=self.sq[:, sl], in_=xb[:, sl], func=AF.Square)
            self.inc(i, "ss")
            self.w(e, "sv", self.v_rows[ch])
            i = e.activation(out=rstd, in_=var, func=AF.Ln, bias=self.epscol[:])
            self.inc(i, "ss")
            self.w(e, "ss", self.v_ln[ch])
            i = e.activation(out=rstd, in_=rstd, func=AF.Exp, scale=-0.5)
            self.inc(i, "ss")

    def em_tensor(self, e):
        rows = self.rows
        for ch in range(NCHUNK):
            self.w(e, "sv", self.v_cast[ch])
            self.w(e, "ss", self.v_sq[ch])
            if ch >= 1:
                self.w(e, "sv", self.v_rows[ch - 1])
            for dt in range(NDT):
                sl = slice(dt * CH, (dt + 1) * CH)
                e.matmul(out=self.ps_s[:], lhsT=self.onesb[:], rhs=self.cb[:, sl],
                         start=(dt == 0), stop=(dt == NDT - 1))
            for dt in range(NDT):
                sl = slice(dt * CH, (dt + 1) * CH)
                i = e.matmul(out=self.ps_q[:], lhsT=self.onesb[:], rhs=self.sq[:, sl],
                             start=(dt == 0), stop=(dt == NDT - 1))
            self.inc(i, "st")
            self.w(e, "sv", self.v_mr[ch])
            if ch >= 1:
                self.w(e, "sv", self.v_out[ch - 1])
            e.matmul(out=self.ps_b[:, 0:CH], lhsT=self.onesf[:],
                     rhs=rows[:, 2 * CH:3 * CH], start=True, stop=True)
            i = e.matmul(out=self.ps_b[:, CH:2 * CH], lhsT=self.onesf[:],
                         rhs=rows[:, 3 * CH:4 * CH], start=True, stop=True)
            self.inc(i, "st")

    def em_gpsimd(self, e):
        for ch in range(NCHUNK):
            self.w(e, "sv", self.v_out[ch])
            xnf, xnb = self.xnf[ch % 2], self.xnb[ch % 2]
            for dt in range(NDT):
                sl = slice(dt * CH, (dt + 1) * CH)
                dsl = (slice(dt * 128, (dt + 1) * 128),
                       slice(ch * CH, (ch + 1) * CH))
                if self.dstf is not None:
                    self.dma(e, self.dstf[dsl], xnf[:, sl], f"Ls{ch % 2}")
                if self.dstb is not None:
                    self.dma(e, self.dstb[dsl], xnb[:, sl], f"Ls{ch % 2}")


class P_RGLRU(Phase):
    def __init__(self, ctx, xnf, s1, w):
        super().__init__(ctx)
        self.xnf, self.s1, self.wdr = xnf, s1, w

    def alloc(self):
        sbuf, psum = self.sb, self.psb
        tag = "rg"
        self.wig = sbuf(tag + "_wig", [128, 8 * 128])
        self.wag = sbuf(tag + "_wag", [128, 8 * 128])
        self.bcols = sbuf(tag + "_bcols", [128, 32])
        self.x = [sbuf(f"{tag}_x{i}", [128, T]) for i in range(2)]
        self.gx = sbuf(tag + "_gx", [128, T])
        self.ga = sbuf(tag + "_ga", [128, T])
        self.la = sbuf(tag + "_la", [128, T])
        self.sh = sbuf(tag + "_sh", [128, T])
        self.ml = sbuf(tag + "_ml", [128, T])
        self.nx = sbuf(tag + "_nx", [128, T])
        self.h = [sbuf(f"{tag}_h{i}", [128, T]) for i in range(2)]
        self.ones = sbuf(tag + "_ones", [128, T])
        self.ps = [psum(f"{tag}_ps{i}", [128, CH]) for i in range(2)]

    def plan(self):
        c = self.ctx
        self.v_wld = c.tick("Rw", 16 * 19)
        self.v_ld = [c.tick(f"Rx{dt % 2}", 16) for dt in range(NDT)]
        self.v_init = c.tick("sv")
        self.v_spe = c.tick("ss")
        self.v_sp = c.tick("ss")
        self.v_sig = []
        self.v_e2, self.v_pexp, self.v_sqm = [], [], []
        for dt in range(NDT):
            self.v_sig.append([c.tick("ss") for _ in range(16)])
            self.v_e2.append(c.tick("ss"))
            self.v_pexp.append(c.tick("ss"))
            self.v_sqm.append(c.tick("ss"))
        self.v_mm = [[c.tick("st") for _ in range(16)] for _ in range(NDT)]
        self.v_la, self.v_shift, self.v_om, self.v_xfree, self.v_h = [], [], [], [], []
        for dt in range(NDT):
            self.v_la.append(c.tick("sv"))
            self.v_shift.append(c.tick("sv"))
            self.v_om.append(c.tick("sv"))
            self.v_xfree.append(c.tick("sv"))
            self.v_h.append(c.tick("sv"))
        self.v_st = [c.tick(f"Rs{dt % 2}", 16) for dt in range(NDT)]

    def em_sync(self, e):
        w = self.wdr
        for hd in range(8):
            self.dma(e, self.wig[:, hd * 128:(hd + 1) * 128], w["igw"][hd], "Rw")
            self.dma(e, self.wag[:, hd * 128:(hd + 1) * 128], w["agw"][hd], "Rw")
        self.dma(e, self.bcols[:, 0:8], w["igb"], "Rw")
        self.dma(e, self.bcols[:, 8:16], w["agb"], "Rw")
        self.dma(e, self.bcols[:, 16:24], w["ap"], "Rw")
        for dt in range(NDT):
            if dt >= 2:
                self.w(e, "sv", self.v_xfree[dt - 2])
            self.dma(e, self.x[dt % 2][:], self.xnf[dt * 128:(dt + 1) * 128, :],
                     f"Rx{dt % 2}")

    def em_scalar(self, e):
        c = self.ctx
        self.w(e, "Rw", self.v_wld)
        self.w(e, "sv", self.v_init)
        ap = self.bcols[:, 16:24]
        onecol = self.bcols[:, 25:26]
        i = e.activation(out=ap, in_=ap, func=AF.Exp)
        self.inc(i, "ss")
        self.w(e, "ss", self.v_spe)
        i = e.activation(out=ap, in_=ap, func=AF.Ln, bias=onecol)
        self.inc(i, "ss")
        for dt in range(NDT):
            for g in range(2):
                gout = self.gx if g == 0 else self.ga
                bcol = (self.bcols[:, dt:dt + 1] if g == 0
                        else self.bcols[:, 8 + dt:9 + dt])
                for chnk in range(NCHUNK):
                    m = g * 8 + chnk
                    self.w(e, "st", self.v_mm[dt][m])
                    i = e.activation(out=gout[:, chnk * CH:(chnk + 1) * CH],
                                     in_=self.ps[chnk % 2][:], func=AF.Sigmoid,
                                     bias=bcol)
                    self.inc(i, "ss")
            self.w(e, "sv", self.v_la[dt])
            i = e.activation(out=self.ml[:], in_=self.la[:], func=AF.Exp, scale=2.0)
            self.inc(i, "ss")
            self.w(e, "sv", self.v_shift[dt])
            i = e.activation(out=self.sh[:], in_=self.sh[:], func=AF.Exp)
            self.inc(i, "ss")
            self.w(e, "sv", self.v_om[dt])
            i = e.activation(out=self.ml[:], in_=self.ml[:], func=AF.Sqrt)
            self.inc(i, "ss")

    def em_tensor(self, e):
        for dt in range(NDT):
            self.w(e, f"Rx{dt % 2}", self.v_ld[dt])
            if dt >= 1:
                self.w(e, "sv", self.v_xfree[dt - 1])
            for g in range(2):
                wsb = self.wig if g == 0 else self.wag
                for chnk in range(NCHUNK):
                    m = g * 8 + chnk
                    if m >= 2:
                        self.w(e, "ss", self.v_sig[dt][m - 2])
                    i = e.matmul(out=self.ps[chnk % 2][:],
                                 lhsT=wsb[:, dt * 128:(dt + 1) * 128],
                                 rhs=self.x[dt % 2][:, chnk * CH:(chnk + 1) * CH],
                                 start=True, stop=True)
                    self.inc(i, "st")

    def em_vector(self, e):
        c = self.ctx
        e.memset(self.ones[:], 1.0)
        self.w(e, "Rw", self.v_wld)
        i = e.memset(self.bcols[:, 25:26], 1.0)
        self.inc(i, "sv")
        for dt in range(NDT):
            self.w(e, "ss", self.v_sig[dt][15])
            i = e.tensor_scalar(out=self.la[:], in0=self.ga[:],
                                scalar1=self.bcols[:, 16 + dt:17 + dt],
                                scalar2=-8.0, op0=ALU.mult, op1=ALU.mult)
            self.inc(i, "sv")
            self.w(e, "ss", self.v_e2[dt])
            e.tensor_tensor_scan(out=self.la[:], data0=self.ones[:],
                                 data1=self.la[:], initial=0.0,
                                 op0=ALU.mult, op1=ALU.add)
            e.memset(self.sh[:, 0:1], 0.0)
            i = e.tensor_copy(out=self.sh[:, 1:T], in_=self.la[:, 0:T - 1])
            self.inc(i, "sv")
            e.tensor_scalar(out=self.ml[:], in0=self.ml[:], scalar1=-1.0,
                            scalar2=1.0, op0=ALU.mult, op1=ALU.add)
            i = e.tensor_scalar(out=self.ml[:], in0=self.ml[:], scalar1=1e-7,
                                scalar2=None, op0=ALU.max)
            self.inc(i, "sv")
            i = e.tensor_mul(out=self.nx[:], in0=self.x[dt % 2][:], in1=self.gx[:])
            self.inc(i, "sv")
            self.w(e, "sv", self.v_xfree[dt])
            self.w(e, "ss", self.v_sqm[dt])
            e.tensor_mul(out=self.nx[:], in0=self.nx[:], in1=self.ml[:])
            self.w(e, "ss", self.v_pexp[dt])
            e.tensor_scalar(out=self.sh[:], in0=self.sh[:], scalar1=1e-7,
                            scalar2=None, op0=ALU.max)
            e.reciprocal(out=self.ml[:], in_=self.sh[:])
            e.tensor_mul(out=self.nx[:], in0=self.nx[:], in1=self.ml[:])
            e.tensor_tensor_scan(out=self.nx[:], data0=self.ones[:],
                                 data1=self.nx[:], initial=0.0,
                                 op0=ALU.mult, op1=ALU.add)
            if dt >= 2:
                self.w(e, f"Rs{dt % 2}", self.v_st[dt - 2])
            i = e.tensor_mul(out=self.h[dt % 2][:], in0=self.sh[:], in1=self.nx[:])
            self.inc(i, "sv")

    def em_gpsimd(self, e):
        for dt in range(NDT):
            self.w(e, "sv", self.v_h[dt])
            self.dma(e, self.s1[dt * 128:(dt + 1) * 128, :], self.h[dt % 2][:],
                     f"Rs{dt % 2}")


def build_program(phase_factories, dram_specs):
    nc = bass.Bass("TRN2", target_bir_lowering=False,
                   detect_race_conditions=False)
    io = {}
    for name, (shape, dt, kind) in dram_specs.items():
        io[name] = nc.dram_tensor(name, list(shape), dt, kind=kind).ap()
    with ExitStack() as stack:
        ctx = Ctx(nc, stack)
        phases = [f(ctx, io) for f in phase_factories]
        block = stack.enter_context(nc.Block())
        for k, ph in enumerate(phases, start=1):
            for eng_name in Ctx.ENGINES:
                def body(e, ph=ph, en=eng_name, k=k):
                    ph.emit(en, e)
                    e.sem_inc(ctx.sem["ph"], 1)
                    e.wait_ge(ctx.sem["ph"], 5 * k)
                getattr(block, eng_name)(body)
            ph.mstack.close()

        def final_body(e):
            for q in sorted(ctx.dma_streams):
                if ctx.cnt[q] > 0:
                    e.wait_ge(ctx.sem[q], ctx.cnt[q])
        block.gpsimd(final_body)
    return nc, io


def _prep_rg_weights(inputs):
    return {
        "igw": np.ascontiguousarray(inputs["L0_attn_ig_w"], np.float32),
        "agw": np.ascontiguousarray(inputs["L0_attn_ag_w"], np.float32),
        "igb": np.ascontiguousarray(np.asarray(inputs["L0_attn_ig_b"], np.float32).T),
        "agb": np.ascontiguousarray(np.asarray(inputs["L0_attn_ag_b"], np.float32).T),
        "ap": np.ascontiguousarray(
            np.asarray(inputs["L0_attn_a_param"], np.float32).reshape(8, 128).T),
    }


class P_ATT0(SPhase):
    def __init__(self, ctx, io, l, mode="full", nw=8, nh=8, bstop=8):
        super().__init__(ctx, io)
        self.l = l
        self.mode = mode
        self.nw = nw
        self.nh = nh
        self.bstop = bstop

    def alloc(self):
        sb, ps = self.sb, self.psb
        t = f"a{self.l}"
        self.wqkvb = sb(t + "wqkvb", [128, 8 * 1536], BF16)
        self.wst = [sb(t + f"wst{i}", [128, 1536]) for i in range(2)]
        self.wob = sb(t + "wob", [128, 4 * 1024], BF16)
        self.tabw = sb(t + "tabw", [128, 2 * 4 * CH])
        self.masks = sb(t + "masks", [128, 8 * 1024])
        self.idb = sb(t + "idb", [128, 128], BF16)
        self.rtb = sb(t + "rtb", [128, 128], BF16)
        self.ebb = sb(t + "ebb", [128, 128], BF16)
        self.qkcols = sb(t + "qkcols", [128, 2])
        self.xw = [sb(t + f"xw{i}", [128, 8 * CH], BF16) for i in range(2)]
        self.rotq = sb(t + "rotq", [128, 2 * 4 * CH], BF16)
        self.rotk = sb(t + "rotk", [128, 2 * 4 * CH], BF16)
        self.vtok = sb(t + "vtok", [128, 2 * 4 * CH], BF16)
        self.vtmp = [sb(t + f"vtmp{i}", [128, CH], BF16) for i in range(2)]
        self.sqb = sb(t + "sqb", [128, 2 * CH], BF16)
        self.rn = sb(t + "rn", [128, 2 * CH])
        self.qh = sb(t + "qh", [128, 2 * CH])
        self.qhb = sb(t + "qhb", [128, 2 * CH], BF16)
        self.t1 = sb(t + "t1", [128, 2 * CH])
        self.simf = sb(t + "simf", [128, 2 * 1024])
        self.Pb = sb(t + "Pb", [128, 2 * 1024], BF16)
        self.PT = sb(t + "PT", [128, 8 * CH], BF16)
        self.ot = sb(t + "ot", [128, 4 * CH], BF16)
        self.negm = sb(t + "negm", [128, 4])
        self.zcol = sb(t + "zcol", [128, 4])
        self.zrec = sb(t + "zrec", [128, 4])
        self.xres = [sb(t + f"xres{i}", [128, CH]) for i in range(2)]
        self.xout = [sb(t + f"xout{i}", [128, CH]) for i in range(2)]
        self.xoutb = [sb(t + f"xoutb{i}", [128, CH], BF16) for i in range(2)]
        self.psq = ps(t + "psq", [128, 4 * CH])
        self.psss = ps(t + "psss", [128, CH])
        self.psrot = ps(t + "psrot", [128, CH])
        self.pstp = ps(t + "pstp", [128, 1024], BF16)
        self.psav = ps(t + "psav", [64, CH])

    def build(self):
        io, l = self.io, self.l
        srcf, srcb = io[f"in{l}f"], io[f"in{l}b"]
        dstf, dstb = io[f"mid{l}f"], io[f"mid{l}b"]
        self.sdma("sync", self.idb[:], io["identb"], q="Ac")
        self.sdma("sync", self.rtb[:], io["rtb"], q="Ac")
        self.sdma("sync", self.ebb[:], io["ebb"], q="Ac")
        self.sdma("sync", self.qkcols[:], io[f"qk{l}"], q="Ac")
        for m in range(8):
            self.sdma("sync", self.masks[:, m * 1024:(m + 1) * 1024],
                      io["masks"][m], q="Ac",
                      key="consts" if m == 7 else None)
        for dt in range(8):
            if dt >= 2:
                self.wt("sync", f"wqb{dt-2}")
            self.sdma("sync", self.wst[dt % 2][:], io[f"wqkv{l}"][dt],
                      q=f"Aw{dt % 2}", key=f"wqf{dt}")
            self.wt("vector", f"wqf{dt}")
            self.op("vector",
                    lambda e, dt=dt: e.tensor_copy(
                        out=self.wqkvb[:, dt * 1536:(dt + 1) * 1536],
                        in_=self.wst[dt % 2][:]),
                    inc="sv", key=f"wqb{dt}")
        for ot_ in range(4):
            if ot_ < 2:
                self.wt("sync", f"wqb{6 + ot_}")
            else:
                self.wt("sync", f"wob{ot_ - 2}")
            self.sdma("sync", self.wst[ot_ % 2][:, 0:1024], io[f"wo{l}"][ot_],
                      q=f"Aw{ot_ % 2}", key=f"wof{ot_}")
            self.wt("vector", f"wof{ot_}")
            self.op("vector",
                    lambda e, ot_=ot_: e.tensor_copy(
                        out=self.wob[:, ot_ * 1024:(ot_ + 1) * 1024],
                        in_=self.wst[ot_ % 2][:, 0:1024]),
                    inc="sv", key=f"wob{ot_}")
        self.op("vector", lambda e: e.memset(self.rotk[:, 4 * CH:8 * CH], 0.0))
        self.op("vector", lambda e: e.memset(self.vtok[:, 4 * CH:8 * CH], 0.0),
                inc="sv", key="previnit")

        for w in range(self.nw):
            par = w % 2
            if w >= 2:
                self.wt("sync", f"qkv{w-2}_11")
            for dt in range(NDT):
                dst = self.xw[par][:, dt * CH:(dt + 1) * CH]
                if dt >= 4:
                    if w == 0:
                        self.sdma("sync", dst[:, 1:CH],
                                  srcb[dt * 128:(dt + 1) * 128, 0:CH - 1],
                                  q=f"Ax{w % 2}",
                                  key=f"xnw{w}" if dt == NDT - 1 else None)
                    else:
                        self.sdma("sync", dst,
                                  srcb[dt * 128:(dt + 1) * 128,
                                       w * CH - 1:(w + 1) * CH - 1],
                                  q=f"Ax{w % 2}",
                                  key=f"xnw{w}" if dt == NDT - 1 else None)
                else:
                    self.sdma("sync", dst,
                              srcb[dt * 128:(dt + 1) * 128, w * CH:(w + 1) * CH],
                              q=f"Ax{w % 2}")
            if w >= 2:
                self.wt("sync", f"rc{w-2}_7")
            for i in range(4):
                self.sdma("sync", self.tabw[:, (par * 4 + i) * CH:(par * 4 + i + 1) * CH],
                          io["tabs"][i][:, w * CH:(w + 1) * CH], q=f"At{w % 2}",
                          key=f"tab{w}" if i == 3 else None)
            if w == 0:
                self.wt("vector", "xnw0")
                def fix0(e):
                    for dt in range(4, NDT):
                        i = e.memset(self.xw[0][:, dt * CH:dt * CH + 1], 0.0)
                    return i
                self.op("vector", fix0, inc="sv", key="xnw0fix")
            self.wt("tensor", f"xnw{w}")
            if w == 0:
                self.wt("tensor", "xnw0fix")
                self.wt("tensor", "wqb7")
                self.wt("tensor", "wob3")
                self.wt("tensor", "consts")
            if w >= 1:
                if self.mode == "full":
                    self.wt("tensor", f"xo{w-1}_7")
                elif self.mode == "attn":
                    self.wt("tensor", f"oev{w-1}_7")
                else:
                    self.wt("tensor", f"vc{w-1}_11")
            for jm in range(12):
                slot = self.psq[:, (jm % 4) * CH:(jm % 4 + 1) * CH]
                if jm >= 4:
                    if jm - 4 < 8:
                        self.wt("tensor", f"qh{w}_{jm-4}")
                    else:
                        self.wt("tensor", f"vc{w}_{jm-4}")
                for dt in range(NDT):
                    self.op("tensor",
                            lambda e, jm=jm, dt=dt, slot=slot, par=par: e.matmul(
                                out=slot,
                                lhsT=self.wqkvb[:, dt * 1536 + jm * 128:
                                                dt * 1536 + (jm + 1) * 128],
                                rhs=self.xw[par][:, dt * CH:(dt + 1) * CH],
                                start=(dt == 0), stop=(dt == NDT - 1)),
                            inc="st" if dt == NDT - 1 else None,
                            key=f"qkv{w}_{jm}" if dt == NDT - 1 else None)
                if jm < 8:
                    p2 = jm % 2
                    isq = jm < 4
                    self.wt("scalar", f"qkv{w}_{jm}")
                    self.op("scalar",
                            lambda e, slot=slot, p2=p2: e.activation(
                                out=self.sqb[:, p2 * CH:(p2 + 1) * CH],
                                in_=slot, func=AF.Square),
                            inc="ss", key=f"sq{w}_{jm}")
                    self.wt("tensor", f"sq{w}_{jm}")
                    if jm >= 1:
                        self.wt("tensor", f"ssc{w}_{jm-1}")
                    self.op("tensor",
                            lambda e, p2=p2: e.matmul(
                                out=self.psss[:], lhsT=self.ebb[:],
                                rhs=self.sqb[:, p2 * CH:(p2 + 1) * CH],
                                start=True, stop=True),
                            inc="st", key=f"ssmm{w}_{jm}")
                    self.wt("vector", f"ssmm{w}_{jm}")
                    self.op("vector",
                            lambda e, p2=p2: e.tensor_scalar(
                                out=self.rn[:, p2 * CH:(p2 + 1) * CH],
                                in0=self.psss[:], scalar1=1e-24, scalar2=None,
                                op0=ALU.max),
                            inc="sv", key=f"ssc{w}_{jm}")
                    self.wt("scalar", f"ssc{w}_{jm}")
                    self.op("scalar",
                            lambda e, p2=p2: e.activation(
                                out=self.rn[:, p2 * CH:(p2 + 1) * CH],
                                in_=self.rn[:, p2 * CH:(p2 + 1) * CH], func=AF.Ln),
                            inc="ss", key=f"rnl{w}_{jm}")
                    self.wt("scalar", f"rnl{w}_{jm}")
                    self.op("scalar",
                            lambda e, p2=p2: e.activation(
                                out=self.rn[:, p2 * CH:(p2 + 1) * CH],
                                in_=self.rn[:, p2 * CH:(p2 + 1) * CH],
                                func=AF.Exp, scale=-0.5),
                            inc="ss", key=f"rn{w}_{jm}")
                    self.wt("vector", f"rn{w}_{jm}")
                    if w == 0:
                        self.wt("vector", "consts")
                    qkcol = self.qkcols[:, 0:1] if isq else self.qkcols[:, 1:2]
                    self.op("vector",
                            lambda e, slot=slot, p2=p2, qkcol=qkcol:
                            e.scalar_tensor_tensor(
                                out=self.qh[:, p2 * CH:(p2 + 1) * CH],
                                in0=slot, scalar=qkcol,
                                in1=self.rn[:, p2 * CH:(p2 + 1) * CH],
                                op0=ALU.mult, op1=ALU.mult),
                            inc="sv", key=f"qh{w}_{jm}")
                    self.wt("vector", f"qh{w}_{jm}")
                    self.op("vector",
                            lambda e, p2=p2: e.tensor_copy(
                                out=self.qhb[:, p2 * CH:(p2 + 1) * CH],
                                in_=self.qh[:, p2 * CH:(p2 + 1) * CH]),
                            inc="sv", key=f"qhb{w}_{jm}")
                    self.wt("tensor", f"qhb{w}_{jm}")
                    if jm >= 1:
                        self.wt("tensor", f"rc{w}_{jm-1}")
                    self.op("tensor",
                            lambda e, p2=p2: e.matmul(
                                out=self.psrot[:], lhsT=self.rtb[:],
                                rhs=self.qhb[:, p2 * CH:(p2 + 1) * CH],
                                start=True, stop=True),
                            inc="st", key=f"rot{w}_{jm}")
                    self.wt("vector", f"rot{w}_{jm}")
                    self.wt("vector", f"tab{w}")
                    cs = self.tabw[:, (par * 4 + (0 if isq else 2)) * CH:
                                   (par * 4 + (0 if isq else 2) + 1) * CH]
                    ss_ = self.tabw[:, (par * 4 + (1 if isq else 3)) * CH:
                                    (par * 4 + (1 if isq else 3) + 1) * CH]
                    self.op("vector",
                            lambda e, p2=p2, cs=cs: e.tensor_mul(
                                out=self.t1[:, p2 * CH:(p2 + 1) * CH],
                                in0=self.qh[:, p2 * CH:(p2 + 1) * CH], in1=cs))
                    self.op("vector",
                            lambda e, p2=p2, ss_=ss_: e.tensor_mul(
                                out=self.qh[:, p2 * CH:(p2 + 1) * CH],
                                in0=self.psrot[:], in1=ss_))
                    self.op("vector",
                            lambda e, p2=p2: e.tensor_add(
                                out=self.t1[:, p2 * CH:(p2 + 1) * CH],
                                in0=self.t1[:, p2 * CH:(p2 + 1) * CH],
                                in1=self.qh[:, p2 * CH:(p2 + 1) * CH]),
                            inc="sv", key=f"rc{w}_{jm}")
                    self.wt("vector", f"rc{w}_{jm}")
                    dstt = self.rotq if isq else self.rotk
                    dcol = (par * 4 + (jm % 4)) * CH
                    if w >= 1 and self.mode != "qkv":
                        self.wt("vector", f"avdone{w-1}")
                    self.op("vector",
                            lambda e, dstt=dstt, dcol=dcol, p2=p2: e.tensor_copy(
                                out=dstt[:, dcol:dcol + CH],
                                in_=self.t1[:, p2 * CH:(p2 + 1) * CH]),
                            inc="sv", key=f"rq{w}_{jm}")
                else:
                    jv = jm - 8
                    p2 = jv % 2
                    self.wt("scalar", f"qkv{w}_{jm}")
                    if jv >= 2:
                        self.wt("scalar", f"vtp{w}_{jv-2}")
                    self.op("scalar",
                            lambda e, slot=slot, p2=p2: e.activation(
                                out=self.vtmp[p2][:], in_=slot, func=AF.Copy),
                            inc="ss", key=f"vc{w}_{jm}")
                    self.wt("tensor", f"vc{w}_{jm}")
                    if w >= 1 and jv == 0 and self.mode != "qkv":
                        self.wt("tensor", f"avdone{w-1}")
                    if jv >= 1:
                        self.wt("tensor", f"vtk{w}_{jv-1}")
                    for tt in range(4):
                        self.op("tensor",
                                lambda e, tt=tt, p2=p2: e.transpose(
                                    out=self.pstp[:, tt * 128:(tt + 1) * 128],
                                    in_=self.vtmp[p2][:, tt * 128:(tt + 1) * 128],
                                    identity=self.idb[:]),
                                inc="st" if tt == 3 else None,
                                key=f"vtp{w}_{jv}" if tt == 3 else None)
                    self.wt("vector", f"vtp{w}_{jv}")
                    if w >= 1 and jv == 0 and self.mode != "qkv":
                        self.wt("vector", f"avdone{w-1}")
                    for tt in range(4):
                        self.op("vector",
                                lambda e, tt=tt, jv=jv, par=par: e.tensor_copy(
                                    out=self.vtok[:, (par * 4 + tt) * CH + jv * 128:
                                                  (par * 4 + tt) * CH + (jv + 1) * 128],
                                    in_=self.pstp[:, tt * 128:(tt + 1) * 128]),
                                inc="sv" if tt == 3 else None,
                                key=f"vtk{w}_{jv}" if tt == 3 else None)
            if self.mode == "qkv":
                if w == self.nw - 1:
                    self.wt("gpsimd", f"rq{w}_7")
                    self.wt("gpsimd", f"vtk{w}_3")
                    self.sdma("gpsimd", self.io["dbg"][0], self.rotq[:], q="Dg")
                    self.sdma("gpsimd", self.io["dbg"][1], self.rotk[:], q="Dg")
                    self.sdma("gpsimd", self.io["dbg"][2], self.vtok[:], q="Dg")
                continue
            ppar = (w + 1) % 2
            for h in range(self.nh):
                jt, rbase = h // 2, (h % 2) * 64
                for qc in range(4):
                    s_ = h * 4 + qc
                    sslot = self.psq[:, (qc % 2) * 1024:(qc % 2) * 1024 + 1024]
                    self.wt("tensor", f"rq{w}_7")
                    if w == 0 and s_ == 0:
                        self.wt("tensor", "previnit")
                    if s_ >= 2:
                        self.wt("tensor", f"mz{w}_{(s_-2)//4}_{(s_-2)%4}")
                    if s_ == 0:
                        self.wt("tensor", f"vc{w}_9")
                    if s_ == 1:
                        self.wt("tensor", f"vc{w}_11")
                    qsl = self.rotq[rbase:rbase + 64,
                                    (par * 4 + jt) * CH + qc * 128:
                                    (par * 4 + jt) * CH + (qc + 1) * 128]
                    for half in range(2):
                        kslot = ppar if half == 0 else par
                        ksl = self.rotk[rbase:rbase + 64,
                                        (kslot * 4 + jt) * CH:(kslot * 4 + jt + 1) * CH]
                        self.op("tensor",
                                lambda e, sslot=sslot, qsl=qsl, ksl=ksl, half=half:
                                e.matmul(out=sslot[:, half * CH:(half + 1) * CH],
                                         lhsT=qsl, rhs=ksl, start=True, stop=True),
                                inc="st" if half == 1 else None,
                                key=(f"sim{w}_{h}_{qc}" if half == 1 else None))
                    if h == 7 and qc == 3:
                        pass
                    if self.bstop <= 1:
                        self.op("tensor", lambda e: e.nop(), inc="st",
                                key=f"mz{w}_{h}_{qc}")
                        continue
                    self.wt("vector", f"sim{w}_{h}_{qc}")
                    if w == 0:
                        self.wt("vector", "consts")
                    if s_ >= 2 and self.bstop >= 3:
                        self.wt("vector", f"pexp{w}_{(s_-2)//4}_{(s_-2)%4}")
                    msl = self.masks[:, (qc + (4 if w == 0 else 0)) * 1024:
                                     (qc + (4 if w == 0 else 0) + 1) * 1024]
                    self.op("vector",
                            lambda e, sslot=sslot, msl=msl, qc=qc: e.tensor_add(
                                out=self.simf[:, (qc % 2) * 1024:(qc % 2 + 1) * 1024],
                                in0=sslot, in1=msl),
                            inc="sv", key=f"mz{w}_{h}_{qc}")
                    self.wt("vector", f"mz{w}_{h}_{qc}")
                    self.op("vector",
                            lambda e, qc=qc: e.reduce_max(
                                out=self.negm[:, qc:qc + 1],
                                in_=self.simf[:, (qc % 2) * 1024:(qc % 2 + 1) * 1024],
                                axis=AX.X, negate=True),
                            inc="sv", key=f"nm{w}_{h}_{qc}")
                    if self.bstop <= 2:
                        continue
                    self.wt("scalar", f"nm{w}_{h}_{qc}")
                    if s_ >= 2 and self.bstop >= 5:
                        self.wt("scalar", f"ptp{w}_{(s_-2)//4}_{(s_-2)%4}")
                    self.op("scalar",
                            lambda e, qc=qc: e.activation(
                                out=self.Pb[:, (qc % 2) * 1024:(qc % 2 + 1) * 1024],
                                in_=self.simf[:, (qc % 2) * 1024:(qc % 2 + 1) * 1024],
                                func=AF.Exp, bias=self.negm[:, qc:qc + 1],
                                accum_out=self.zcol[:, qc:qc + 1]),
                            inc="ss", key=f"pexp{w}_{h}_{qc}")
                    if self.bstop <= 3:
                        continue
                    self.wt("vector", f"pexp{w}_{h}_{qc}")
                    self.op("vector",
                            lambda e, qc=qc: e.reciprocal(
                                out=self.zrec[:, qc:qc + 1],
                                in_=self.zcol[:, qc:qc + 1]),
                            inc="sv", key=f"zr{w}_{h}_{qc}")
                    self.wt("vector", f"zr{w}_{h}_{qc}")
                    self.op("vector",
                            lambda e, qc=qc: e.tensor_scalar(
                                out=self.Pb[:, (qc % 2) * 1024:(qc % 2 + 1) * 1024],
                                in0=self.Pb[:, (qc % 2) * 1024:(qc % 2 + 1) * 1024],
                                scalar1=self.zrec[:, qc:qc + 1], scalar2=None,
                                op0=ALU.mult),
                            inc="sv", key=f"pn{w}_{h}_{qc}")
                    if self.bstop <= 4:
                        continue
                    self.wt("tensor", f"pn{w}_{h}_{qc}")
                    if s_ >= 1 and self.bstop >= 6:
                        self.wt("tensor", f"ptk{w}_{(s_-1)//4}_{(s_-1)%4}")
                    elif w >= 1:
                        pass
                    for kt in range(8):
                        self.op("tensor",
                                lambda e, kt=kt, qc=qc: e.transpose(
                                    out=self.pstp[:, (kt % 8) * 128:(kt % 8 + 1) * 128],
                                    in_=self.Pb[:, (qc % 2) * 1024 + kt * 128:
                                                (qc % 2) * 1024 + (kt + 1) * 128],
                                    identity=self.idb[:]),
                                inc="st" if kt == 7 else None,
                                key=f"ptp{w}_{h}_{qc}" if kt == 7 else None)
                    if self.bstop <= 5:
                        continue
                    self.wt("vector", f"ptp{w}_{h}_{qc}")
                    if s_ == 0 and w >= 1:
                        self.wt("vector", f"avdone{w-1}")
                    if qc == 0 and h >= 1 and self.bstop >= 7:
                        self.wt("vector", f"av{w}_{h-1}")
                    for kt in range(8):
                        self.op("vector",
                                lambda e, kt=kt, qc=qc: e.tensor_copy(
                                    out=self.PT[:, kt * CH + qc * 128:
                                                kt * CH + (qc + 1) * 128],
                                    in_=self.pstp[:, kt * 128:(kt + 1) * 128]),
                                inc="sv" if kt == 7 else None,
                                key=f"ptk{w}_{h}_{qc}" if kt == 7 else None)
                if self.bstop <= 6:
                    continue
                self.wt("tensor", f"ptk{w}_{h}_3")
                if w == 0 and h == 0:
                    self.wt("tensor", "vtk0_3")
                if h >= 1 and self.bstop >= 8:
                    self.wt("tensor", f"oev{w}_{h-1}")
                for kt in range(8):
                    vslot = ppar if kt < 4 else par
                    tt = kt % 4
                    vsl = self.vtok[:, (vslot * 4 + tt) * CH + jt * 128 + rbase:
                                    (vslot * 4 + tt) * CH + jt * 128 + rbase + 64]
                    self.op("tensor",
                            lambda e, kt=kt, vsl=vsl: e.matmul(
                                out=self.psav[:], lhsT=vsl,
                                rhs=self.PT[:, kt * CH:(kt + 1) * CH],
                                start=(kt == 0), stop=(kt == 7)),
                            inc="st" if kt == 7 else None,
                            key=(f"av{w}_{h}" if kt == 7 else None))
                if h == 7:
                    self.alias(f"avdone{w}", f"av{w}_7")
                if self.bstop <= 7:
                    continue
                self.wt("vector", f"av{w}_{h}")
                self.op("vector",
                        lambda e, jt=jt, rbase=rbase: e.tensor_copy(
                            out=self.ot[rbase:rbase + 64, jt * CH:(jt + 1) * CH],
                            in_=self.psav[0:64, :]),
                        inc="sv", key=f"oev{w}_{h}")
            self.op("vector", lambda e: e.engine_nop(), inc="sv",
                    key=f"simlast{w}")
            if self.mode == "attn":
                if w == self.nw - 1 and self.bstop >= 8 and self.nh == 8:
                    self.wt("gpsimd", f"oev{w}_7")
                    self.sdma("gpsimd", self.io["dbg"][0][:, 0:2048], self.ot[:, 0:2048], q="Dg")
                continue
            for dm in range(8):
                n_ = w * 8 + dm
                if n_ >= 2:
                    self.wt("sync", f"xo{(n_-2)//8}_{(n_-2)%8}")
                self.sdma("sync", self.xres[dm % 2][:],
                          srcf[dm * 128:(dm + 1) * 128, w * CH:(w + 1) * CH],
                          q=f"Ar{dm % 2}", key=f"xr{w}_{dm}")
                self.wt("tensor", f"oev{w}_7")
                if dm >= 1:
                    self.wt("tensor", f"xo{w}_{dm-1}")
                for ot_ in range(4):
                    self.op("tensor",
                            lambda e, ot_=ot_, dm=dm: e.matmul(
                                out=self.psq[:, 0:CH],
                                lhsT=self.wob[:, ot_ * 1024 + dm * 128:
                                              ot_ * 1024 + (dm + 1) * 128],
                                rhs=self.ot[:, ot_ * CH:(ot_ + 1) * CH],
                                start=(ot_ == 0), stop=(ot_ == 3)),
                            inc="st" if ot_ == 3 else None,
                            key=f"wo{w}_{dm}" if ot_ == 3 else None)
                self.wt("vector", f"wo{w}_{dm}")
                self.wt("vector", f"xr{w}_{dm}")
                if w * 8 + dm >= 2:
                    n_ = w * 8 + dm - 2
                    self.wt("vector", f"xst{n_//8}_{n_%8}")
                self.op("vector",
                        lambda e, dm=dm: e.tensor_add(
                            out=self.xout[dm % 2][:], in0=self.psq[:, 0:CH],
                            in1=self.xres[dm % 2][:]))
                self.op("vector",
                        lambda e, dm=dm: e.tensor_copy(
                            out=self.xoutb[dm % 2][:], in_=self.xout[dm % 2][:]),
                        inc="sv", key=f"xo{w}_{dm}")
                self.wt("gpsimd", f"xo{w}_{dm}")
                self.sdma("gpsimd", dstf[dm * 128:(dm + 1) * 128, w * CH:(w + 1) * CH],
                          self.xout[dm % 2][:], q=f"As{dm % 2}")
                self.sdma("gpsimd", dstb[dm * 128:(dm + 1) * 128, w * CH:(w + 1) * CH],
                          self.xoutb[dm % 2][:], q=f"As{dm % 2}", key=f"xst{w}_{dm}")


def _l0_tables():
    half = np.arange(0, 64, 2, dtype=np.float32)
    inv_freq = (1.0 / 10000.0 ** (half / 64.0)).astype(np.float32)
    t = np.arange(T, dtype=np.float32)
    freqs = t[:, None] * inv_freq[None, :]
    freqs = np.concatenate([freqs, freqs], axis=-1)
    base = ((half + 25.6) / 89.6).astype(np.float32)
    power = ((t - T // 2) / 256.0).astype(np.float32)
    scale = base[None, :] ** power[:, None]
    scale = np.concatenate([scale, scale], axis=-1).astype(np.float32)
    cos, sin = np.cos(freqs), np.sin(freqs)
    def tile2(a):
        return np.ascontiguousarray(np.tile(a.T.astype(np.float32), (2, 1)))
    return np.stack([tile2(cos * scale * 8.0), tile2(sin * scale * 8.0),
                     tile2(cos / scale), tile2(sin / scale)])


def _l0_masks():
    NEG = -np.float32(np.finfo(np.float32).max)
    masks = np.zeros((8, 128, 1024), np.float32)
    jj = np.arange(1024)
    for qc in range(4):
        ii = 128 * qc + np.arange(128)[:, None]
        bad = (jj[None, :] < ii) | (jj[None, :] > ii + 512)
        masks[qc] = np.where(bad, NEG, 0.0)
        masks[4 + qc] = np.where(bad | (jj[None, :] < 512), NEG, 0.0)
    return masks


def _blockdiags():
    rt = np.zeros((64, 64), np.float32)
    for i in range(32):
        rt[i, i + 32] = 1.0
        rt[i + 32, i] = -1.0
    z = np.zeros_like(rt)
    rtb = np.block([[rt, z], [z, rt]]).astype(ml_dtypes.bfloat16)
    eb = np.zeros((128, 128), np.float32)
    eb[:64, :64] = 1.0
    eb[64:, 64:] = 1.0
    ebb = eb.astype(ml_dtypes.bfloat16)
    identb = np.eye(128, dtype=np.float32).astype(ml_dtypes.bfloat16)
    return rtb, ebb, identb


def _prep_att0(inputs, l):
    wqkv = np.ascontiguousarray(
        np.asarray(inputs["L0_ffn_wqkv"][l], np.float32).reshape(8, 128, 1536))
    wo = np.ascontiguousarray(
        np.asarray(inputs["L0_ffn_wo"][l], np.float32).reshape(4, 128, 1024))
    qs = np.asarray(inputs["L0_ffn_q_scale"][l], np.float32)
    ks = np.asarray(inputs["L0_ffn_k_scale"][l], np.float32)
    qk = np.stack([np.tile(qs, 2), np.tile(ks, 2)], axis=1)
    return wqkv, wo, np.ascontiguousarray(qk)


class P_MLP(SPhase):
    NJ = 11
    NJT = 22

    def __init__(self, ctx, io, l, nch=NCHUNK, mstop=4):
        super().__init__(ctx, io)
        self.l = l
        self.nch = nch
        self.mstop = mstop

    def alloc(self):
        sb, ps = self.sb, self.psb
        t = f"m{self.l}"
        self.w1b = sb(t + "w1b", [128, 8 * 5460], BF16)
        self.w2st = [sb(t + f"w2st{i}", [128, CH]) for i in range(2)]
        self.w2b = [sb(t + f"w2b{i}", [128, CH], BF16) for i in range(2)]
        self.b1r = sb(t + "b1r", [1, 5460], BF16)
        self.idb2 = sb(t + "idb2", [128, 128], BF16)
        self.b1st = sb(t + "b1st", [1, 5460])
        self.ones1 = sb(t + "ones1", [1, 128], BF16)
        self.b2c = sb(t + "b2c", [128, 8])
        self.xln = [sb(t + f"xln{i}", [128, 8 * CH], BF16) for i in range(2)]
        self.h = sb(t + "h", [128, 5460])
        self.hhb = [sb(t + f"hhb{i}", [128, 2816], BF16) for i in range(2)]
        self.xres = [sb(t + f"xres{i}", [128, CH]) for i in range(2)]
        self.xout = [sb(t + f"xout{i}", [128, CH]) for i in range(2)]
        self.xoutb = [sb(t + f"xoutb{i}", [128, CH], BF16) for i in range(2)]
        self.wstack = ExitStack()
        self.wst = [self.wstack.enter_context(
            self.ctx.nc.sbuf_tensor(t + "wst0", [128, 5460], F32))]
        self.wstack.close()
        self.hhT = sb(t + "hhT", [128, self.NJT * CH], BF16)
        self.ps_h = ps(t + "ps_h", [128, 2 * CH])
        self.ps_tp = ps(t + "ps_tp", [128, 16 * 128], BF16)
        self.ps_w2 = ps(t + "ps_w2", [128, 4 * CH])

    def build(self):
        io, l = self.io, self.l
        xlnb, srcf = io[f"xln{l}"], io[f"mid{l}f"]
        dstf, dstb = io[f"xf{l}f"], io[f"xf{l}b"]
        w1, b1, w2h, b2 = io[f"w1_{l}"], io[f"b1_{l}"], io[f"w2h_{l}"], io[f"b2_{l}"]
        NJ, NJT = self.NJ, self.NJT

        self.sdma("sync", self.b1st[:], b1, q="Mc")
        self.sdma("sync", self.idb2[:], io["identb"], q="Mc")
        self.sdma("sync", self.b2c[:], b2, q="Mc", key="mconsts")
        self.op("vector", lambda e: e.memset(self.ones1[:], 1.0))
        self.wt("vector", "mconsts")
        self.op("vector", lambda e: e.tensor_copy(out=self.b1r[:], in_=self.b1st[:]),
                inc="sv", key="b1cast")
        for dt in range(8):
            if dt >= 1:
                self.wt("sync", f"m1b{dt-1}")
            self.sdma("sync", self.wst[0][:], w1[dt], q=f"Mw{dt % 2}",
                      key=f"m1f{dt}")
            self.wt("vector", f"m1f{dt}")
            self.op("vector",
                    lambda e, dt=dt: e.tensor_copy(
                        out=self.w1b[:, dt * 5460:(dt + 1) * 5460],
                        in_=self.wst[0][:]),
                    inc="sv", key=f"m1b{dt}")

        for ch in range(self.nch):
            par = ch % 2
            if ch >= 2:
                self.wt("sync", f"w1last{ch-2}")
            for dt in range(8):
                self.sdma("sync", self.xln[par][:, dt * CH:(dt + 1) * CH],
                          xlnb[dt * 128:(dt + 1) * 128, ch * CH:(ch + 1) * CH],
                          q=f"Mx{par}", key=f"mxl{ch}" if dt == 7 else None)
            self.wt("tensor", f"mxl{ch}")
            if ch == 0:
                self.wt("tensor", "m1b7")
                self.wt("tensor", "b1cast")
            for tM in range(4):
                if self.mstop in (7, 8, 9):
                    self.alias(f"hv{ch}_{tM}_{NJ-1}", f"mxl{ch}")
                for nch in range(NJ if self.mstop not in (7, 8, 9) else 0):
                    nsz = min(CH, 5460 - CH * nch)
                    p2 = (tM * NJ + nch) % 2
                    slot = self.ps_h[:, p2 * CH:p2 * CH + nsz]
                    k = ch * 44 + tM * NJ + nch
                    if k >= 2:
                        kk = k - 2
                        self.wt("tensor", f"hv{kk // 44}_{(kk % 44) // NJ}_{(kk % 44) % NJ}")
                    for dt in range(8):
                        self.op("tensor",
                                lambda e, dt=dt, tM=tM, nch=nch, nsz=nsz, slot=slot, par=par:
                                e.matmul(
                                    out=slot,
                                    lhsT=self.xln[par][:, dt * CH + tM * 128:
                                                       dt * CH + (tM + 1) * 128],
                                    rhs=self.w1b[:, dt * 5460 + nch * CH:
                                                 dt * 5460 + nch * CH + nsz],
                                    start=(dt == 0), stop=False))
                    self.op("tensor",
                            lambda e, nch=nch, nsz=nsz, slot=slot: e.matmul(
                                out=slot, lhsT=self.ones1[:],
                                rhs=self.b1r[:, nch * CH:nch * CH + nsz],
                                start=False, stop=True),
                            inc="st", key=f"w1mm{ch}_{tM}_{nch}")
                    self.wt("vector", f"w1mm{ch}_{tM}_{nch}")
                    self.op("vector",
                            lambda e, nch=nch, nsz=nsz, slot=slot: e.tensor_copy(
                                out=self.h[:, nch * CH:nch * CH + nsz], in_=slot),
                            inc="sv", key=f"hv{ch}_{tM}_{nch}")
                if self.mstop <= 1:
                    continue
                if self.mstop == 8:
                    self.alias(f"hh{ch}_{tM}", f"mxl{ch}")
                    self.alias(f"gel{ch}_{tM}", f"mxl{ch}")
                if self.mstop != 8:
                    self.wt("scalar", f"hv{ch}_{tM}_{NJ-1}")
                    self.op("scalar",
                            lambda e: e.activation(out=self.h[:, 2730:5460],
                                                   in_=self.h[:, 2730:5460],
                                                   func=AF.Gelu),
                            inc="ss", key=f"gel{ch}_{tM}")
                if self.mstop != 8:
                    self.wt("vector", f"gel{ch}_{tM}")
                n_ = ch * 4 + tM
                if n_ >= 2 and self.mstop > 2:
                    self.wt("vector", f"tpd{(n_-2)//4}_{(n_-2)%4}")
                if self.mstop != 8:
                    self.op("vector",
                            lambda e, tM=tM: e.tensor_mul(
                                out=self.hhb[tM % 2][:, 0:2730], in0=self.h[:, 0:2730],
                                in1=self.h[:, 2730:5460]),
                            inc="sv", key=f"hh{ch}_{tM}")
                if self.mstop <= 2:
                    continue
                self.wt("tensor", f"hh{ch}_{tM}")
                if ch == 0 and tM == 0:
                    self.wt("tensor", "mconsts")
                if ch >= 1 and tM == 0:
                    self.wt("tensor", f"w2ev{ch-1}_7")
                NJT_eff = 20 if self.mstop == 5 else NJT
                ngrp = (NJT_eff + 7) // 8
                for g_ in range(ngrp):
                    jts = list(range(g_ * 8, min(NJT_eff, g_ * 8 + 8)))
                    if g_ >= 2:
                        self.wt("tensor", f"tpe{ch}_{tM}_{g_-2}")
                    elif g_ == 0 and (ch * 4 + tM) > 0:
                        n_ = ch * 4 + tM - 1
                        self.wt("tensor", f"tpe{n_//4}_{n_%4}_{max(0, (NJT_eff + 7) // 8 - 2)}")
                    for jt in jts:
                        rows = min(128, 2730 - jt * 128)
                        self.op("tensor",
                                lambda e, jt=jt, rows=rows, tM=tM, g_=g_: e.transpose(
                                    out=self.ps_tp[0:rows, (g_ % 2) * 1024 + (jt % 8) * 128:
                                                   (g_ % 2) * 1024 + (jt % 8) * 128 + 128],
                                    in_=self.hhb[tM % 2][:, jt * 128:jt * 128 + rows],
                                    identity=self.idb2[:]),
                                inc="st" if jt == jts[-1] else None,
                                key=f"tp{ch}_{tM}_{g_}" if jt == jts[-1] else None)
                    self.wt("vector", f"tp{ch}_{tM}_{g_}")
                    for jt in jts:
                        rows = min(128, 2730 - jt * 128)
                        self.op("vector",
                                lambda e, jt=jt, rows=rows, tM=tM, g_=g_: e.tensor_copy(
                                    out=self.hhT[0:rows, jt * CH + tM * 128:
                                                 jt * CH + (tM + 1) * 128],
                                    in_=self.ps_tp[0:rows, (g_ % 2) * 1024 + (jt % 8) * 128:
                                                   (g_ % 2) * 1024 + (jt % 8) * 128 + 128]),
                                inc="sv" if jt == jts[-1] else None,
                                key=f"tpe{ch}_{tM}_{g_}" if jt == jts[-1] else None)
                self.alias(f"tpd{ch}_{tM}", f"tpe{ch}_{tM}_{ngrp - 1}")
            if self.mstop <= 2:
                self.alias(f"w1last{ch}", f"hv{ch}_3_{NJ-1}" if self.mstop > 1
                           else f"hv{ch}_3_{NJ-1}")
            else:
                self.alias(f"w1last{ch}", f"tpd{ch}_3")
            if self.mstop <= 3:
                continue
            for p in range(2):
                for jt in range(NJT):
                    rows = min(128, 2730 - jt * 128)
                    jj = ch * 44 + p * NJT + jt
                    if jj >= 2:
                        kk = jj - 2
                        self.wt("sync", f"w2c{kk // 44}_{(kk % 44) // NJT}_{(kk % 44) % NJT}")
                    self.sdma("sync", self.w2st[jt % 2][0:rows, :],
                              w2h[p][jt][0:rows], q=f"Mv{jt % 2}",
                              key=f"w2f{ch}_{p}_{jt}")
                    self.wt("scalar", f"w2f{ch}_{p}_{jt}")
                    if jj >= 2:
                        kk = jj - 2
                        self.wt("scalar", f"w2mm{kk // 44}_{(kk % 44) // NJT}_{(kk % 44) % NJT}")
                    self.op("scalar",
                            lambda e, jt=jt, rows=rows: e.activation(
                                out=self.w2b[jt % 2][0:rows, :],
                                in_=self.w2st[jt % 2][0:rows, :], func=AF.Copy),
                            inc="ss", key=f"w2c{ch}_{p}_{jt}")
                    self.wt("tensor", f"w2c{ch}_{p}_{jt}")
                    self.wt("tensor", f"tpd{ch}_3")
                    if jt == 0 and (ch > 0 or p > 0):
                        pk = (ch * 2 + p) - 1
                        self.wt("tensor", f"w2ev{pk // 2}_{(pk % 2) * 4 + 3}")
                    for dmq in range(4):
                        self.op("tensor",
                                lambda e, jt=jt, rows=rows, dmq=dmq: e.matmul(
                                    out=self.ps_w2[:, dmq * CH:(dmq + 1) * CH],
                                    lhsT=self.w2b[jt % 2][0:rows, dmq * 128:(dmq + 1) * 128],
                                    rhs=self.hhT[0:rows, jt * CH:(jt + 1) * CH],
                                    start=(jt == 0), stop=(jt == NJT - 1)),
                                inc="st" if dmq == 3 else None,
                                key=f"w2mm{ch}_{p}_{jt}" if dmq == 3 else None)
                for dmq in range(4):
                    dm = p * 4 + dmq
                    n_ = ch * 8 + dm
                    if n_ >= 2:
                        self.wt("sync", f"mxo{(n_-2)//8}_{(n_-2)%8}")
                    self.sdma("sync", self.xres[dmq % 2][:],
                              srcf[dm * 128:(dm + 1) * 128, ch * CH:(ch + 1) * CH],
                              q=f"Mr{dmq % 2}", key=f"mxr{ch}_{dm}")
                    self.wt("vector", f"w2mm{ch}_{p}_{NJT-1}")
                    self.wt("vector", f"mxr{ch}_{dm}")
                    if n_ >= 2:
                        self.wt("vector", f"mst{(n_-2)//8}_{(n_-2)%8}")
                    self.op("vector",
                            lambda e, dmq=dmq, dm=dm: e.scalar_tensor_tensor(
                                out=self.xout[dmq % 2][:],
                                in0=self.ps_w2[:, dmq * CH:(dmq + 1) * CH],
                                scalar=self.b2c[:, dm:dm + 1],
                                in1=self.xres[dmq % 2][:],
                                op0=ALU.add, op1=ALU.add))
                    self.op("vector",
                            lambda e, dmq=dmq: e.tensor_copy(
                                out=self.xoutb[dmq % 2][:], in_=self.xout[dmq % 2][:]),
                            inc="sv", key=f"mxo{ch}_{dm}")
                    self.wt("gpsimd", f"mxo{ch}_{dm}")
                    self.sdma("gpsimd", dstf[dm * 128:(dm + 1) * 128, ch * CH:(ch + 1) * CH],
                              self.xout[dmq % 2][:], q=f"Mo{dmq % 2}")
                    self.sdma("gpsimd", dstb[dm * 128:(dm + 1) * 128, ch * CH:(ch + 1) * CH],
                              self.xoutb[dmq % 2][:], q=f"Mo{dmq % 2}",
                              key=f"mst{ch}_{dm}")
                self.alias(f"w2ev{ch}_{p * 4 + 3}", f"mxo{ch}_{p * 4 + 3}")


def _prep_mlp(inputs, l):
    w1 = np.ascontiguousarray(
        np.asarray(inputs["L0_ffn_w1"][l], np.float32).reshape(8, 128, 5460))
    b1 = np.ascontiguousarray(np.asarray(inputs["L0_ffn_b1"][l], np.float32)
                              .reshape(1, 5460))
    w2 = np.asarray(inputs["L0_ffn_w2"][l], np.float32)
    w2p = np.zeros((2816, 1024), np.float32)
    w2p[:2730] = w2
    w2h = np.zeros((2, 22, 128, 512), np.float32)
    for p in range(2):
        for jt in range(22):
            w2h[p, jt] = w2p[jt * 128:(jt + 1) * 128, p * 512:(p + 1) * 512]
    b2 = np.ascontiguousarray(np.asarray(inputs["L0_ffn_b2"][l], np.float32)
                              .reshape(8, 128).T)
    lng = np.ascontiguousarray(np.asarray(inputs["L0_ffn_ln_g"][l], np.float32)
                               .reshape(8, 128).T)
    lnb = np.ascontiguousarray(np.asarray(inputs["L0_ffn_ln_b"][l], np.float32)
                               .reshape(8, 128).T)
    return w1, b1, w2h, b2, lng, lnb


class P_COMB(SPhase):
    def alloc(self):
        sb, ps = self.sb, self.psb
        t = "cb"
        self.xcb = [sb(t + f"x{i}", [128, NDT * CH]) for i in range(2)]
        self.scb = sb(t + "s", [128, 2 * CH])
        self.gcb = sb(t + "g", [128, 2 * CH])
        self.xb = [sb(t + f"xb{i}", [128, NDT * CH]) for i in range(2)]
        self.cb = sb(t + "cb", [128, NDT * CH], BF16)
        self.sq = sb(t + "sq", [128, NDT * CH], BF16)
        self.onesb = sb(t + "onesb", [128, 1], BF16)
        self.onesf = sb(t + "onesf", [1, 128])
        self.idf = sb(t + "idf", [128, 128])
        self.rows = sb(t + "rows", [1, 4 * CH])
        self.epscol = sb(t + "eps", [1, 1])
        self.xnb = [sb(t + f"xnb{i}", [128, NDT * CH], BF16) for i in range(2)]
        self.tokc = sb(t + "tokc", [128, 4 * 1024])
        self.tmpn = sb(t + "tmpn", [128, CH])
        self.ps_s = ps(t + "pss", [1, CH])
        self.ps_q = ps(t + "psq", [1, CH])
        self.ps_b = ps(t + "psb", [128, 2 * CH])
        self.ps_t = ps(t + "pst", [128, 2 * CH])

    def build(self):
        io = self.io
        xsrc, ssrc, gsrc = io["x_inT"], io["s1"], io["g1"]
        dstb, tokdst = io["xn2b"], io["x2tok"]
        self.op("vector", lambda e: e.memset(self.onesb[:], 1.0))
        self.op("vector", lambda e: e.memset(self.onesf[:], 1.0))
        self.op("vector", lambda e: e.memset(self.epscol[:], 1e-5), inc="sv",
                key="cinit")
        self.sdma("sync", self.idf[:], io["identf"], q="Cc", key="cid")
        for ch in range(NCHUNK):
            par = ch % 2
            if ch >= 2:
                self.wt("sync", f"cxn{ch-2}")
            for dt in range(NDT):
                self.sdma("sync", self.xcb[par][:, dt * CH:(dt + 1) * CH],
                          xsrc[dt * 128:(dt + 1) * 128, ch * CH:(ch + 1) * CH],
                          q=f"Ca{par}", key=f"cx{ch}" if dt == NDT - 1 else None)
            for dt in range(NDT):
                n_ = ch * 8 + dt
                if n_ >= 2:
                    kk = n_ - 2
                    self.wt("sync", f"cmb{kk//8}_{kk%8}")
                self.sdma("sync", self.scb[:, (dt % 2) * CH:(dt % 2 + 1) * CH],
                          ssrc[dt * 128:(dt + 1) * 128, ch * CH:(ch + 1) * CH],
                          q=f"Cs{dt % 2}", key=f"cs{ch}_{dt}")
                self.sdma("sync", self.gcb[:, (dt % 2) * CH:(dt % 2 + 1) * CH],
                          gsrc[dt * 128:(dt + 1) * 128, ch * CH:(ch + 1) * CH],
                          q=f"Cg{dt % 2}", key=f"cg{ch}_{dt}")
                self.wt("scalar", f"cg{ch}_{dt}")
                self.op("scalar",
                        lambda e, dt=dt: e.activation(
                            out=self.gcb[:, (dt % 2) * CH:(dt % 2 + 1) * CH],
                            in_=self.gcb[:, (dt % 2) * CH:(dt % 2 + 1) * CH],
                            func=AF.Silu),
                        inc="ss", key=f"csil{ch}_{dt}")
                self.wt("vector", f"cs{ch}_{dt}")
                self.wt("vector", f"csil{ch}_{dt}")
                self.wt("vector", f"cx{ch}")
                self.op("vector",
                        lambda e, dt=dt: e.tensor_mul(
                            out=self.scb[:, (dt % 2) * CH:(dt % 2 + 1) * CH],
                            in0=self.scb[:, (dt % 2) * CH:(dt % 2 + 1) * CH],
                            in1=self.gcb[:, (dt % 2) * CH:(dt % 2 + 1) * CH]))
                self.op("vector",
                        lambda e, dt=dt, par=par: e.tensor_add(
                            out=self.xb[par][:, dt * CH:(dt + 1) * CH],
                            in0=self.xcb[par][:, dt * CH:(dt + 1) * CH],
                            in1=self.scb[:, (dt % 2) * CH:(dt % 2 + 1) * CH]),
                        inc="sv", key=f"cmb{ch}_{dt}")
            for dt in range(NDT):
                self.wt("vector", f"cmb{ch}_{dt}")
                self.op("vector",
                        lambda e, dt=dt, par=par: e.tensor_copy(
                            out=self.cb[:, dt * CH:(dt + 1) * CH],
                            in_=self.xb[par][:, dt * CH:(dt + 1) * CH]),
                        inc="sv" if dt == NDT - 1 else None,
                        key=f"ccast{ch}" if dt == NDT - 1 else None)
                self.wt("scalar", f"cmb{ch}_{dt}")
                self.op("scalar",
                        lambda e, dt=dt, par=par: e.activation(
                            out=self.sq[:, dt * CH:(dt + 1) * CH],
                            in_=self.xb[par][:, dt * CH:(dt + 1) * CH],
                            func=AF.Square),
                        inc="ss" if dt == NDT - 1 else None,
                        key=f"csq{ch}" if dt == NDT - 1 else None)
            self.wt("tensor", f"ccast{ch}")
            self.wt("tensor", f"csq{ch}")
            if ch >= 1:
                self.wt("tensor", f"crows{ch-1}")
            for dt in range(NDT):
                self.op("tensor",
                        lambda e, dt=dt: e.matmul(
                            out=self.ps_s[:], lhsT=self.onesb[:],
                            rhs=self.cb[:, dt * CH:(dt + 1) * CH],
                            start=(dt == 0), stop=(dt == NDT - 1)))
            for dt in range(NDT):
                self.op("tensor",
                        lambda e, dt=dt: e.matmul(
                            out=self.ps_q[:], lhsT=self.onesb[:],
                            rhs=self.sq[:, dt * CH:(dt + 1) * CH],
                            start=(dt == 0), stop=(dt == NDT - 1)),
                        inc="st" if dt == NDT - 1 else None,
                        key=f"cstat{ch}" if dt == NDT - 1 else None)
            rows = self.rows
            mu, var = rows[:, 0:CH], rows[:, CH:2 * CH]
            rstd, mr = rows[:, 2 * CH:3 * CH], rows[:, 3 * CH:4 * CH]
            self.wt("vector", f"cstat{ch}")
            self.wt("vector", "cinit")
            self.op("vector", lambda e: e.tensor_scalar(
                out=mu, in0=self.ps_s[:], scalar1=1.0 / D, scalar2=None,
                op0=ALU.mult))
            self.op("vector", lambda e: e.tensor_scalar(
                out=var, in0=self.ps_q[:], scalar1=1.0 / D, scalar2=None,
                op0=ALU.mult))
            self.op("vector", lambda e: e.tensor_mul(out=mr, in0=mu, in1=mu))
            self.op("vector", lambda e: e.tensor_sub(out=var, in0=var, in1=mr),
                    inc="sv", key=f"crows{ch}")
            self.wt("scalar", f"crows{ch}")
            self.op("scalar", lambda e: e.activation(out=rstd, in_=var,
                                                     func=AF.Ln,
                                                     bias=self.epscol[:]),
                    inc="ss", key=f"cln{ch}")
            self.wt("scalar", f"cln{ch}")
            self.op("scalar", lambda e: e.activation(out=rstd, in_=rstd,
                                                     func=AF.Exp, scale=-0.5),
                    inc="ss", key=f"crstd{ch}")
            self.wt("vector", f"crstd{ch}")
            self.op("vector", lambda e: e.tensor_mul(out=mr, in0=mu, in1=rstd),
                    inc="sv", key=f"cmr{ch}")
            self.wt("tensor", f"cmr{ch}")
            if ch >= 1:
                self.wt("tensor", f"cxn{ch-1}")
            self.op("tensor", lambda e: e.matmul(
                out=self.ps_b[:, 0:CH], lhsT=self.onesf[:],
                rhs=self.rows[:, 2 * CH:3 * CH], start=True, stop=True))
            self.op("tensor", lambda e: e.matmul(
                out=self.ps_b[:, CH:2 * CH], lhsT=self.onesf[:],
                rhs=self.rows[:, 3 * CH:4 * CH], start=True, stop=True),
                inc="st", key=f"cbcast{ch}")
            self.wt("vector", f"cbcast{ch}")
            if ch >= 2:
                self.wt("vector", f"cxst{ch-2}")
            for dt in range(NDT):
                sl = slice(dt * CH, (dt + 1) * CH)
                self.op("vector",
                        lambda e, sl=sl, par=par: e.tensor_mul(
                            out=self.tmpn[:], in0=self.xb[par][:, sl],
                            in1=self.ps_b[:, 0:CH]))
                self.op("vector",
                        lambda e: e.tensor_sub(
                            out=self.tmpn[:], in0=self.tmpn[:],
                            in1=self.ps_b[:, CH:2 * CH]))
                self.op("vector",
                        lambda e, sl=sl, par=par: e.tensor_copy(
                            out=self.xnb[par][:, sl], in_=self.tmpn[:]),
                        inc="sv" if sl.start == (NDT - 1) * CH else None,
                        key=f"cxn{ch}" if sl.start == (NDT - 1) * CH else None)
            self.wt("gpsimd", f"cxn{ch}")
            for dt in range(NDT):
                self.sdma("gpsimd", dstb[dt * 128:(dt + 1) * 128, ch * CH:(ch + 1) * CH],
                          self.xnb[par][:, dt * CH:(dt + 1) * CH], q=f"Cn{par}",
                          key=f"cxst{ch}" if dt == NDT - 1 else None)
            for dt in range(NDT):
                self.wt("tensor", f"cmb{ch}_{dt}")
                if ch == 0 and dt == 0:
                    self.wt("tensor", "cid")
                n_ = ch * 8 + dt
                if n_ >= 2:
                    kk = n_ - 2
                    self.wt("tensor", f"ctpe{kk//8}_{kk%8}")
                for tt in range(4):
                    self.op("tensor",
                            lambda e, dt=dt, tt=tt, par=par: e.transpose(
                                out=self.ps_t[:, (dt % 2) * CH + tt * 128:
                                              (dt % 2) * CH + (tt + 1) * 128],
                                in_=self.xb[par][:, dt * CH + tt * 128:
                                                dt * CH + (tt + 1) * 128],
                                identity=self.idf[:]),
                            inc="st" if tt == 3 else None,
                            key=f"ctp{ch}_{dt}" if tt == 3 else None)
                self.wt("vector", f"ctp{ch}_{dt}")
                if dt == 0 and ch >= 1:
                    self.wt("vector", f"ctst{ch-1}")
                for tt in range(4):
                    self.op("vector",
                            lambda e, dt=dt, tt=tt: e.tensor_copy(
                                out=self.tokc[:, tt * 1024 + dt * 128:
                                              tt * 1024 + (dt + 1) * 128],
                                in_=self.ps_t[:, (dt % 2) * CH + tt * 128:
                                              (dt % 2) * CH + (tt + 1) * 128]),
                            inc="sv" if tt == 3 else None,
                            key=f"ctpe{ch}_{dt}" if tt == 3 else None)
            self.wt("gpsimd", f"ctpe{ch}_{NDT-1}")
            for tt in range(4):
                self.sdma("gpsimd",
                          tokdst[ch * CH + tt * 128:ch * CH + (tt + 1) * 128, :],
                          self.tokc[:, tt * 1024:(tt + 1) * 1024], q=f"Ct{tt % 2}",
                          key=f"ctst{ch}" if tt == 3 else None)


class P_L1W(SPhase):
    def alloc(self):
        sb, ps = self.sb, self.psb
        t = "lw"
        self.stg = [sb(t + f"stg{i}", [128, 1024]) for i in range(2)]
        self.wqb = [sb(t + f"wqb{i}", [128, 1024], BF16) for i in range(2)]
        self.wkb = sb(t + "wkb", [128, 16 * 64], BF16)
        self.wvb = sb(t + "wvb", [128, 16 * 64], BF16)
        self.kst = sb(t + "kst", [128, 8 * 64])
        self.kdb = sb(t + "kdb", [128, 8 * 64], BF16)
        self.oq = sb(t + "oq", [128, 8 * 64], BF16)
        self.ov = sb(t + "ov", [64, 1024], BF16)
        self.ps1 = ps(t + "ps1", [128, 2 * 512])
        self.ps2 = ps(t + "ps2", [64, 2 * 512])

    def build(self):
        io = self.io
        self.sdma("sync", self.kst[:, 0:512], io["wkup"].rearrange(
            "a b c -> b (a c)")[:, 0:1024][:, 0:512], q="LwS") if False else None
        for qt in range(16):
            self.sdma("sync", self.kst[:, 0:64], io["wkup"][qt], q="LwS",
                      key=f"lk{qt}")
            self.wt("vector", f"lk{qt}")
            self.op("vector", lambda e, qt=qt: e.tensor_copy(
                out=self.wkb[:, qt * 64:(qt + 1) * 64], in_=self.kst[:, 0:64]),
                inc="sv", key=f"lkc{qt}")
            self.wt("sync", f"lkc{qt}")
            self.sdma("sync", self.kst[:, 0:64], io["wvup"][qt], q="LwS",
                      key=f"lv{qt}")
            self.wt("vector", f"lv{qt}")
            self.op("vector", lambda e, qt=qt: e.tensor_copy(
                out=self.wvb[:, qt * 64:(qt + 1) * 64], in_=self.kst[:, 0:64]),
                inc="sv", key=f"lvc{qt}")
            self.wt("sync", f"lvc{qt}")
        for dt in range(8):
            self.sdma("sync", self.kst[:, dt * 64:(dt + 1) * 64], io["kdt"][dt],
                      q="LwS", key=f"ld{dt}")
        self.wt("vector", "ld7")
        self.op("vector", lambda e: e.tensor_copy(out=self.kdb[:], in_=self.kst[:]),
                inc="sv", key="ldc")
        self.wt("gpsimd", "ldc")
        self.sdma("gpsimd", io["kdb_d"], self.kdb[:], q="LwO")
        for h in range(8):
            for qs in range(2):
                qt = 2 * h + qs
                if qt >= 2:
                    self.wt("sync", f"lqc{qt-2}")
                    self.wt("sync", f"lqm{h-1}_7")
                self.sdma("sync", self.stg[qt % 2][:], io["wq16"][qt],
                          q=f"LwQ{qt % 2}", key=f"lq{qt}")
                self.wt("vector", f"lq{qt}")
                self.op("vector", lambda e, qt=qt: e.tensor_copy(
                    out=self.wqb[qt % 2][:], in_=self.stg[qt % 2][:]),
                    inc="sv", key=f"lqc{qt}")
            for dmt in range(8):
                self.wt("tensor", f"lqc{2*h+1}")
                self.wt("tensor", "lkc15")
                n_ = h * 8 + dmt
                if n_ >= 2:
                    kk = n_ - 2
                    self.wt("tensor", f"lqe{kk//8}_{kk%8}")
                for qs in range(2):
                    self.op("tensor",
                            lambda e, h=h, qs=qs, dmt=dmt: e.matmul(
                                out=self.ps1[:, (dmt % 2) * 512:(dmt % 2) * 512 + 64],
                                lhsT=self.wqb[qs][:, dmt * 128:(dmt + 1) * 128],
                                rhs=self.wkb[:, (2 * h + qs) * 64:(2 * h + qs + 1) * 64],
                                start=(qs == 0), stop=(qs == 1)),
                            inc="st" if qs == 1 else None,
                            key=f"lqm{h}_{dmt}" if qs == 1 else None)
                self.wt("scalar", f"lqm{h}_{dmt}")
                self.op("scalar",
                        lambda e, h=h, dmt=dmt: e.activation(
                            out=self.oq[:, dmt * 64:(dmt + 1) * 64],
                            in_=self.ps1[:, (dmt % 2) * 512:(dmt % 2) * 512 + 64],
                            func=AF.Copy, scale=0.0625),
                        inc="ss", key=f"lqe{h}_{dmt}")
            self.wt("gpsimd", f"lqe{h}_7")
            for dmt in range(8):
                self.sdma("gpsimd", io["wqk_d"][dmt][:, h * 64:(h + 1) * 64],
                          self.oq[:, dmt * 64:(dmt + 1) * 64], q="LwO",
                          key=f"lqs{h}" if dmt == 7 else None)
            if h < 7:
                self.wt("scalar", f"lqs{h}") if False else None
                self.wt("scalar", f"lqs{h}")
        for h in range(8):
            for qs in range(2):
                qt = 2 * h + qs
                if qt >= 2:
                    self.wt("sync", f"lwc{qt-2}")
                    self.wt("sync", f"lvm{h-1}_1")
                else:
                    self.wt("sync", f"lqc{15}")
                    self.wt("sync", "lqm7_7")
                self.sdma("sync", self.stg[qt % 2][:], io["wot16"][qt],
                          q=f"LwQ{qt % 2}", key=f"lw{qt}")
                self.wt("vector", f"lw{qt}")
                self.op("vector", lambda e, qt=qt: e.tensor_copy(
                    out=self.wqb[qt % 2][:], in_=self.stg[qt % 2][:]),
                    inc="sv", key=f"lwc{qt}")
            for c2 in range(2):
                self.wt("tensor", f"lwc{2*h+1}")
                self.wt("tensor", "lvc15")
                n_ = h * 2 + c2
                if n_ >= 2:
                    kk = n_ - 2
                    self.wt("tensor", f"lve{kk//2}_{kk%2}")
                for qs in range(2):
                    self.op("tensor",
                            lambda e, h=h, qs=qs, c2=c2: e.matmul(
                                out=self.ps2[:, (c2 % 2) * 512:(c2 % 2 + 1) * 512],
                                lhsT=self.wvb[:, (2 * h + qs) * 64:(2 * h + qs + 1) * 64],
                                rhs=self.wqb[qs][:, c2 * 512:(c2 + 1) * 512],
                                start=(qs == 0), stop=(qs == 1)),
                            inc="st" if qs == 1 else None,
                            key=f"lvm{h}_{c2}" if qs == 1 else None)
                self.wt("vector", f"lvm{h}_{c2}")
                self.op("vector",
                        lambda e, c2=c2: e.tensor_copy(
                            out=self.ov[:, c2 * 512:(c2 + 1) * 512],
                            in_=self.ps2[:, (c2 % 2) * 512:(c2 % 2 + 1) * 512]),
                        inc="sv", key=f"lve{h}_{c2}")
            self.wt("gpsimd", f"lve{h}_1")
            self.sdma("gpsimd",
                      io["wvo_d"][h // 2][(h % 2) * 64:(h % 2) * 64 + 64, :],
                      self.ov[:], q="LwO", key=f"lvs{h}")
            if h < 7:
                self.wt("vector", f"lvs{h}")


class P_L1A(SPhase):
    def __init__(self, ctx, io=None, nh=8, lstop=9):
        super().__init__(ctx, io)
        self.nh = nh
        self.lstop = lstop

    def alloc(self):
        sb, ps = self.sb, self.psb
        t = "la"
        self.xn = sb(t + "xn", [128, 8 * T], BF16)
        self.wqk = sb(t + "wqk", [128, 8 * 512], BF16)
        self.wvo = sb(t + "wvo", [128, 4 * 1024], BF16)
        self.kdb = sb(t + "kdb", [128, 8 * 64], BF16)
        self.latx = sb(t + "latx", [65, T], BF16)
        self.ltok = sb(t + "ltok", [128, 32 * 65], BF16)
        self.mcb = sb(t + "mcb", [64, 2])
        self.mcbb = sb(t + "mcbb", [64, 1], BF16)
        self.q2 = sb(t + "q2", [65, T], BF16)
        self.mh = sb(t + "mh", [1, T])
        self.absq = sb(t + "absq", [64, 2 * CH], BF16)
        self.PT = sb(t + "PT", [128, 2 * CH], BF16)
        self.cst = sb(t + "cst", [128, 4 * T], BF16)
        self.zrt = sb(t + "zrt", [1, T])
        self.zg = sb(t + "zg", [128, 8 * 32])
        self.zt = sb(t + "zt", [32, 128])
        self.zrow = sb(t + "zrow", [1, T])
        self.bbb = sb(t + "bbb", [128, CH], BF16)
        self.onesf = sb(t + "onesf", [1, 64])
        self.onec = sb(t + "onec", [1, 1])
        self.s2c = [sb(t + f"s2c{i}", [128, 1024]) for i in range(2)]
        self.idb = sb(t + "idb", [128, 128], BF16)
        self.idf = sb(t + "idf", [128, 128])
        self.ps_sc = ps(t + "ps_sc", [128, 2 * CH])
        self.ps_av = ps(t + "ps_av", [65, 2 * CH])
        self.ps_c = ps(t + "ps_c", [128, CH])
        self.ps_c2 = ps(t + "ps_c2", [128, CH])
        self.ps_t = [ps(t + f"ps_t{i}", [128, 256], BF16) for i in range(2)]

    def build(self):
        io = self.io
        self.op("vector", lambda e: e.memset(self.onesf[:], 1.0))
        self.op("vector", lambda e: e.memset(self.onec[:], 1.0))
        self.op("vector", lambda e: e.memset(self.ltok[:], 1.0))
        self.op("vector", lambda e: e.memset(self.latx[64:65, :], 1.0),
                inc="sv", key="linit")
        for dt in range(8):
            self.sdma("sync", self.xn[:, dt * T:(dt + 1) * T],
                      io["xn2b"][dt * 128:(dt + 1) * 128, :], q=f"Px{dt % 2}",
                      key=f"pxn{dt}")
        for dmt in range(8):
            self.sdma("sync", self.wqk[:, dmt * 512:(dmt + 1) * 512],
                      io["wqk_d"][dmt], q="Pw")
        for kt in range(4):
            self.sdma("sync", self.wvo[:, kt * 1024:(kt + 1) * 1024],
                      io["wvo_d"][kt], q="Pw")
        self.sdma("sync", self.idb[:], io["identb"], q="Pw")
        self.sdma("sync", self.idf[:], io["identf"], q="Pw")
        self.sdma("sync", self.kdb[:], io["kdb_d"], q="Pw", key="pw")
        for c in range(8):
            self.wt("tensor", "pxn7")
            self.wt("tensor", "pw")
            if c >= 2:
                self.wt("tensor", f"lev{c-2}")
            for dt in range(8):
                self.op("tensor",
                        lambda e, c=c, dt=dt: e.matmul(
                            out=self.ps_av[0:64, (c % 2) * CH:(c % 2 + 1) * CH],
                            lhsT=self.kdb[:, dt * 64:(dt + 1) * 64],
                            rhs=self.xn[:, dt * T + c * CH:dt * T + (c + 1) * CH],
                            start=(dt == 0), stop=(dt == 7)),
                        inc="st" if dt == 7 else None,
                        key=f"lmm{c}" if dt == 7 else None)
            self.wt("vector", f"lmm{c}")
            self.wt("vector", "linit")
            self.op("vector",
                    lambda e, c=c: e.tensor_copy(
                        out=self.latx[0:64, c * CH:(c + 1) * CH],
                        in_=self.ps_av[0:64, (c % 2) * CH:(c % 2 + 1) * CH]),
                    inc="sv", key=f"lev{c}")
        self.wt("vector", "lev7")
        self.op("vector",
                lambda e: e.reduce_max(out=self.mcb[:, 0:1], in_=self.latx[0:64, :],
                                       axis=AX.X, apply_absolute_value=True),
                inc="sv", key="pmc")
        self.wt("vector", "pmc")
        self.op("vector", lambda e: e.tensor_copy(out=self.mcbb[:], in_=self.mcb[:, 0:1]),
                inc="sv", key="pmcb")
        for kt in range(32):
            self.wt("tensor", f"lev{7}")
            if kt >= 2:
                self.wt("tensor", f"ltk{kt-2}")
            self.op("tensor",
                    lambda e, kt=kt: e.transpose(
                        out=self.ps_t[kt % 2][:, 0:64],
                        in_=self.latx[0:64, kt * 128:(kt + 1) * 128],
                        identity=self.idb[0:64, 0:64]),
                    inc="st", key=f"ltp{kt}")
            self.wt("vector", f"ltp{kt}")
            self.op("vector",
                    lambda e, kt=kt: e.tensor_copy(
                        out=self.ltok[:, kt * 65:kt * 65 + 64],
                        in_=self.ps_t[kt % 2][:, 0:64]),
                    inc="sv", key=f"ltk{kt}")
        if self.lstop <= 1:
            return
        for h in range(self.nh):
            for c in range(8):
                n_ = h * 8 + c
                self.wt("tensor", f"sfree{h-1}" if c == 0 and h >= 1 else "pw")
                if n_ >= 2:
                    kk = n_ - 2
                    self.wt("tensor", f"q2e{kk//8}_{kk%8}")
                for dmt in range(8):
                    self.op("tensor",
                            lambda e, h=h, c=c, dmt=dmt: e.matmul(
                                out=self.ps_av[0:64, (c % 2) * CH:(c % 2 + 1) * CH],
                                lhsT=self.wqk[:, dmt * 512 + h * 64:dmt * 512 + (h + 1) * 64],
                                rhs=self.xn[:, dmt * T + c * CH:dmt * T + (c + 1) * CH],
                                start=(dmt == 0), stop=(dmt == 7)),
                            inc="st" if dmt == 7 else None,
                            key=f"q2m{h}_{c}" if dmt == 7 else None)
                self.wt("scalar", f"q2m{h}_{c}")
                if h * 8 + c >= 2:
                    kk = h * 8 + c - 2
                    self.wt("scalar", f"mhm{kk//8}_{kk%8}")
                self.op("scalar",
                        lambda e, c=c: e.activation(
                            out=self.q2[0:64, c * CH:(c + 1) * CH],
                            in_=self.ps_av[0:64, (c % 2) * CH:(c % 2 + 1) * CH],
                            func=AF.Copy),
                        inc="ss", key=f"q2c{h}_{c}")
                self.op("scalar",
                        lambda e, c=c: e.activation(
                            out=self.absq[:, (c % 2) * CH:(c % 2 + 1) * CH],
                            in_=self.ps_av[0:64, (c % 2) * CH:(c % 2 + 1) * CH],
                            func=AF.Abs),
                        inc="ss", key=f"q2a{h}_{c}")
                self.wt("tensor", f"q2a{h}_{c}")
                self.wt("tensor", "pmcb")
                psm = self.ps_c if c % 2 == 0 else self.ps_c2
                self.op("tensor",
                        lambda e, c=c, psm=psm: e.matmul(
                            out=psm[0:1, :], lhsT=self.mcbb[:],
                            rhs=self.absq[:, (c % 2) * CH:(c % 2 + 1) * CH],
                            start=True, stop=True),
                        inc="st", key=f"mhm{h}_{c}")
                self.wt("vector", f"mhm{h}_{c}")
                self.op("vector",
                        lambda e, c=c, psm=psm: e.tensor_copy(
                            out=self.mh[:, c * CH:(c + 1) * CH],
                            in_=psm[0:1, :]),
                        inc="sv", key=f"q2e{h}_{c}")
            self.wt("vector", f"q2e{h}_7")
            self.op("vector",
                    lambda e: e.tensor_scalar(out=self.q2[64:65, :], in0=self.mh[:],
                                              scalar1=-1.0, scalar2=None,
                                              op0=ALU.mult),
                    inc="sv", key=f"q2r{h}")
            if self.lstop <= 2:
                self.alias(f"sfree{h}", f"q2r{h}")
                continue
            for qq in range(8):
                for kt in range(32):
                    self.wt("tensor", f"q2r{h}")
                    if kt >= 2:
                        self.wt("tensor", f"pex{h}_{qq}_{kt-2}")
                    self.op("tensor",
                            lambda e, h=h, qq=qq, kt=kt: e.matmul(
                                out=self.ps_sc[:, (kt % 2) * CH:(kt % 2 + 1) * CH],
                                lhsT=self.latx[:, kt * 128:(kt + 1) * 128],
                                rhs=self.q2[:, qq * CH:(qq + 1) * CH],
                                start=True, stop=True),
                            inc="st", key=f"psc{h}_{qq}_{kt}")
                    self.wt("scalar", f"psc{h}_{qq}_{kt}")
                    if kt >= 2:
                        self.wt("scalar", f"pav{h}_{qq}_{kt-2}")
                    self.op("scalar",
                            lambda e, kt=kt: e.activation(
                                out=self.PT[:, (kt % 2) * CH:(kt % 2 + 1) * CH],
                                in_=self.ps_sc[:, (kt % 2) * CH:(kt % 2 + 1) * CH],
                                func=AF.Exp),
                            inc="ss", key=f"pex{h}_{qq}_{kt}")
                    self.wt("tensor", f"pex{h}_{qq}_{kt}")
                    if kt == 0 and qq >= 2:
                        self.wt("tensor", f"pcz{h}_{qq-2}")
                    self.op("tensor",
                            lambda e, kt=kt, qq=qq: e.matmul(
                                out=self.ps_av[:, (qq % 2) * CH:(qq % 2 + 1) * CH],
                                lhsT=self.ltok[:, kt * 65:(kt + 1) * 65],
                                rhs=self.PT[:, (kt % 2) * CH:(kt % 2 + 1) * CH],
                                start=(kt == 0), stop=(kt == 31)),
                            inc="st", key=f"pav{h}_{qq}_{kt}")
                self.wt("vector", f"pav{h}_{qq}_{31}")
                jt, rb = h // 2, (h % 2) * 64
                self.op("vector",
                        lambda e, jt=jt, rb=rb, qq=qq: e.tensor_copy(
                            out=self.cst[rb:rb + 64, jt * T + qq * CH:jt * T + (qq + 1) * CH],
                            in_=self.ps_av[0:64, (qq % 2) * CH:(qq % 2 + 1) * CH]))
                self.op("vector",
                        lambda e, qq=qq: e.tensor_copy(
                            out=self.zrt[:, qq * CH:(qq + 1) * CH],
                            in_=self.ps_av[64:65, (qq % 2) * CH:(qq % 2 + 1) * CH]),
                        inc="sv", key=f"pcz{h}_{qq}")
            if self.lstop <= 3:
                self.alias(f"sfree{h}", f"pcz{h}_7")
                continue
            for pz in range(32):
                self.wt("tensor", f"pcz{h}_{pz//4}")
                if pz >= 1 or h >= 1:
                    n_ = h * 32 + pz - 1
                    self.wt("tensor", f"pzg{n_//32}_{n_%32}")
                psm = self.ps_c if pz % 2 == 0 else self.ps_c2
                self.op("tensor",
                        lambda e, pz=pz, psm=psm: e.matmul(
                            out=psm[:, 0:1],
                            lhsT=self.zrt[:, pz * 128:(pz + 1) * 128],
                            rhs=self.onec[:], start=True, stop=True),
                        inc="st", key=f"pzt{h}_{pz}")
                self.wt("vector", f"pzt{h}_{pz}")
                self.op("vector",
                        lambda e, h=h, pz=pz, psm=psm: e.tensor_copy(
                            out=self.zg[:, h * 32 + pz:h * 32 + pz + 1],
                            in_=psm[:, 0:1]),
                        inc="sv", key=f"pzg{h}_{pz}")
            self.op("tensor", lambda e: e.ldweights(self.mcbb[:]),
                    inc=None) if False else None
            self.alias(f"sfree{h}", f"pzg{h}_{31}")
        if self.lstop <= 4 or self.nh < 8:
            return
        self.wt("vector", f"pzg{7}_{31}")
        self.op("vector", lambda e: e.reciprocal(out=self.zg[:], in_=self.zg[:]),
                inc="sv", key="pzr")
        for h in range(8):
            self.wt("tensor", "pzr")
            if h >= 2:
                self.wt("tensor", f"pzte{h-2}")
            psm = self.ps_c if h % 2 == 0 else self.ps_c2
            self.op("tensor",
                    lambda e, h=h, psm=psm: e.transpose(
                        out=psm[0:32, 0:128],
                        in_=self.zg[:, h * 32:(h + 1) * 32],
                        identity=self.idf[:]),
                    inc="st", key=f"pztr{h}")
            self.wt("vector", f"pztr{h}")
            if h >= 1:
                self.wt("vector", f"pzs{h-1}")
            self.op("vector",
                    lambda e, psm=psm: e.tensor_copy(out=self.zt[:],
                                                     in_=psm[0:32, 0:128]),
                    inc="sv", key=f"pzte{h}")
            self.wt("gpsimd", f"pzte{h}")
            self.sdma("gpsimd", io["zb"][h].rearrange("(a b) -> a b", b=128),
                      self.zt[:], q="Pz", key=f"pzs{h}")
        if self.lstop <= 5:
            return
        for h in range(8):
            self.wt("sync", f"pzs{h}")
            if h >= 1:
                self.wt("sync", f"pnb{h-1}_7")
            self.sdma("sync", self.zrow[:], io["zb"][h:h + 1], q="Pz2",
                      key=f"pzl{h}")
            jt, rb = h // 2, (h % 2) * 64
            for c in range(8):
                self.wt("tensor", f"pzl{h}")
                if h == 0 and c == 0:
                    self.wt("tensor", "pzte7")
                n_ = h * 8 + c
                if n_ >= 1:
                    kk = n_ - 1
                    self.wt("tensor", f"pnr{kk//8}_{kk%8}")
                psm = self.ps_c if c % 2 == 0 else self.ps_c2
                self.op("tensor",
                        lambda e, c=c, psm=psm: e.matmul(
                            out=psm[0:64, :], lhsT=self.onesf[:],
                            rhs=self.zrow[:, c * CH:(c + 1) * CH],
                            start=True, stop=True),
                        inc="st", key=f"pnb{h}_{c}")
                self.wt("vector", f"pnb{h}_{c}")
                self.op("vector",
                        lambda e, psm=psm, rb=rb: e.tensor_copy(
                            out=self.bbb[rb:rb + 64, :], in_=psm[0:64, :]))
                self.op("vector",
                        lambda e, jt=jt, rb=rb, c=c: e.tensor_mul(
                            out=self.cst[rb:rb + 64, jt * T + c * CH:jt * T + (c + 1) * CH],
                            in0=self.cst[rb:rb + 64, jt * T + c * CH:jt * T + (c + 1) * CH],
                            in1=self.bbb[rb:rb + 64, :]),
                        inc="sv", key=f"pnr{h}_{c}")
        if self.lstop <= 6:
            return
        for qM in range(32):
            par = qM % 2
            self.wt("tensor", f"pnr7_7")
            if qM >= 1:
                self.wt("tensor", f"ps2e{qM-1}")
            for half in range(2):
                for kt in range(4):
                    self.op("tensor",
                            lambda e, qM=qM, half=half, kt=kt: e.matmul(
                                out=self.ps_sc[:, half * CH:(half + 1) * CH],
                                lhsT=self.cst[:, kt * T + qM * 128:kt * T + (qM + 1) * 128],
                                rhs=self.wvo[:, kt * 1024 + half * 512:
                                             kt * 1024 + (half + 1) * 512],
                                start=(kt == 0), stop=(kt == 3)),
                            inc="st" if (kt == 3 and half == 1) else None,
                            key=f"ps2m{qM}" if (kt == 3 and half == 1) else None)
            self.wt("vector", f"ps2m{qM}")
            if qM >= 2:
                self.wt("vector", f"ps2s{qM-2}")
            self.op("vector",
                    lambda e, par=par: e.tensor_copy(
                        out=self.s2c[par][:], in_=self.ps_sc[:, 0:1024]),
                    inc="sv", key=f"ps2e{qM}")
            self.wt("gpsimd", f"ps2e{qM}")
            self.sdma("gpsimd", io["s2tok"][qM * 128:(qM + 1) * 128, :],
                      self.s2c[par][:], q=f"Ps{par}", key=f"ps2s{qM}")


def _prep_l1(inputs):
    wq16 = np.ascontiguousarray(
        np.asarray(inputs["L1_attn_Wq"], np.float32).reshape(16, 128, 1024))
    wkup = np.ascontiguousarray(
        np.asarray(inputs["L1_attn_Wk_up"], np.float32).reshape(16, 128, 64))
    wvup = np.ascontiguousarray(
        np.asarray(inputs["L1_attn_Wv_up"], np.float32).reshape(16, 128, 64))
    wot16 = np.ascontiguousarray(
        np.asarray(inputs["L1_attn_Wo"], np.float32).T.reshape(16, 128, 1024))
    kdt = np.ascontiguousarray(
        np.asarray(inputs["L1_attn_Wkv_down"], np.float32).T.reshape(8, 128, 64))
    return wq16, wkup, wvup, wot16, kdt


class P_F1A(SPhase):
    def __init__(self, ctx, io=None, astop=9):
        self.astop = astop
        super().__init__(ctx, io)

    def alloc(self):
        sb, ps = self.sb, self.psb
        t = "fa"
        self.seb = sb(t + "seb", [128, 8 * T], BF16)
        self.xnb = sb(t + "xnb", [128, 8 * T], BF16)
        self.stg = [sb(t + "stg0", [128, T])]
        self.sq = sb(t + "sq", [128, T], BF16)
        self.frow = sb(t + "frow", [1, 2 * T])
        self.gcols = sb(t + "gcols", [128, 16])
        self.onesb = sb(t + "onesb", [128, 1], BF16)
        self.onesf = sb(t + "onesf", [1, 128])
        self.idb = sb(t + "idb", [128, 128], BF16)
        self.bbb = sb(t + "bbb", [128, CH], BF16)
        self.At = sb(t + "At", [128, 2 * CH], BF16)
        self.ATt = sb(t + "ATt", [128, 2 * CH], BF16)
        self.mdacc = sb(t + "mdacc", [128, T], BF16)
        self.mcg = sb(t + "mcg", [128, 32])
        self.mdg = sb(t + "mdg", [128, 32])
        self.mtmp = sb(t + "mtmp", [128, 1])
        self.mgt = sb(t + "mgt", [32, 128], BF16)
        self.tokc = sb(t + "tokc", [128, 1024], BF16)
        self.idf2 = sb(t + "idf2", [128, 128])
        self.ps_a = ps(t + "ps_a", [128, 2 * CH])
        self.ps_t = ps(t + "ps_t", [128, 4 * CH], BF16)
        self.ps_r = ps(t + "ps_r", [33, CH])
        self.ps_b = ps(t + "ps_b", [128, CH])
        self.ps_b2 = ps(t + "ps_b2", [128, CH])

    def build(self):
        io = self.io
        self.op("vector", lambda e: e.memset(self.onesb[:], 1.0))
        self.op("vector", lambda e: e.memset(self.onesf[:], 1.0))
        self.op("vector", lambda e: e.memset(self.mdacc[:], -3.0e38))
        self.op("vector", lambda e: e.memset(self.mcg[:], -3.0e38),
                inc="sv", key="finit")
        self.sdma("sync", self.idb[:], io["identb"], q="Fc")
        self.sdma("sync", self.idf2[:], io["identf"], q="Fc")
        self.sdma("sync", self.gcols[:, 0:8], io["gsc"], q="Fc")
        self.sdma("sync", self.gcols[:, 8:16], io["gxc"], q="Fc", key="fgc")
        self.wt("vector", "fgc")
        for w, srcname, buf, go, ro in (("se", "seT", self.seb, 0, 0),
                                        ("xn", "xn2b", self.xnb, 8, 1)):
            src = io[srcname]
            for dt in range(8):
                if w == "se":
                    if dt >= 1:
                        self.wt("sync", f"{w}cast{dt-1}")
                    self.sdma("sync", self.stg[0][:],
                              src[dt * 128:(dt + 1) * 128, :], q=f"Fs{dt % 2}",
                              key=f"{w}ld{dt}")
                    self.wt("vector", f"{w}ld{dt}")
                    self.op("vector",
                            lambda e, dt=dt, buf=buf: e.tensor_copy(
                                out=buf[:, dt * T:(dt + 1) * T],
                                in_=self.stg[0][:]),
                            inc="sv", key=f"{w}cast{dt}")
                else:
                    self.sdma("sync", buf[:, dt * T:(dt + 1) * T],
                              src[dt * 128:(dt + 1) * 128, :], q=f"Fs{dt % 2}",
                              key=f"{w}cast{dt}")
            for dt in range(8):
                self.wt("scalar", f"{w}cast{dt}")
                if dt >= 1:
                    self.wt("scalar", f"{w}sm{dt-1}_7")
                elif w == "xn":
                    self.wt("scalar", "sesm7_7")
                self.op("scalar",
                        lambda e, dt=dt, buf=buf: e.activation(
                            out=self.sq[:], in_=buf[:, dt * T:(dt + 1) * T],
                            func=AF.Square),
                        inc="ss", key=f"{w}sqc{dt}")
                for c in range(8):
                    self.wt("tensor", f"{w}sqc{dt}")
                    if dt * 8 + c >= 1:
                        kk = dt * 8 + c - 1
                        self.wt("tensor", f"{w}sa{kk//8}_{kk%8}")
                    elif w == "xn":
                        self.wt("tensor", "sesa7_7")
                    self.op("tensor",
                            lambda e, c=c, dt=dt: e.matmul(
                                out=self.ps_r[0:1, :], lhsT=self.onesb[:],
                                rhs=self.sq[:, c * CH:(c + 1) * CH],
                                start=True, stop=True),
                            inc="st", key=f"{w}sm{dt}_{c}")
                    self.wt("vector", f"{w}sm{dt}_{c}")
                    if dt == 0:
                        self.op("vector",
                                lambda e, c=c, ro=ro: e.tensor_copy(
                                    out=self.frow[:, ro * T + c * CH:ro * T + (c + 1) * CH],
                                    in_=self.ps_r[0:1, :]),
                                inc="sv", key=f"{w}sa{dt}_{c}")
                    else:
                        self.op("vector",
                                lambda e, c=c, ro=ro: e.tensor_add(
                                    out=self.frow[:, ro * T + c * CH:ro * T + (c + 1) * CH],
                                    in0=self.frow[:, ro * T + c * CH:ro * T + (c + 1) * CH],
                                    in1=self.ps_r[0:1, :]),
                                inc="sv", key=f"{w}sa{dt}_{c}")
                if True:
                    pass
            self.wt("vector", f"{w}sa7_7")
            self.op("vector",
                    lambda e, ro=ro: e.tensor_scalar(
                        out=self.frow[:, ro * T:(ro + 1) * T],
                        in0=self.frow[:, ro * T:(ro + 1) * T], scalar1=1e-24,
                        scalar2=None, op0=ALU.max),
                    inc="sv", key=f"{w}mx")
            self.wt("scalar", f"{w}mx")
            self.op("scalar",
                    lambda e, ro=ro: e.activation(
                        out=self.frow[:, ro * T:(ro + 1) * T],
                        in_=self.frow[:, ro * T:(ro + 1) * T], func=AF.Ln),
                    inc="ss", key=f"{w}ln")
            self.wt("scalar", f"{w}ln")
            self.op("scalar",
                    lambda e, ro=ro: e.activation(
                        out=self.frow[:, ro * T:(ro + 1) * T],
                        in_=self.frow[:, ro * T:(ro + 1) * T], func=AF.Exp,
                        scale=-0.5),
                    inc="ss", key=f"{w}ex")
            self.wt("vector", f"{w}ex")
            self.op("vector",
                    lambda e, ro=ro: e.tensor_scalar(
                        out=self.frow[:, ro * T:(ro + 1) * T],
                        in0=self.frow[:, ro * T:(ro + 1) * T], scalar1=32.0,
                        scalar2=None, op0=ALU.mult),
                    inc="sv", key=f"{w}f")
            for dt in range(8):
                for c in range(8):
                    n_ = dt * 8 + c
                    self.wt("tensor", f"{w}f")
                    if n_ >= 2:
                        kk = n_ - 2
                        self.wt("tensor", f"{w}sc{kk//8}_{kk%8}")
                    psb = self.ps_b if c % 2 == 0 else self.ps_b2
                    self.op("tensor",
                            lambda e, c=c, ro=ro, psb=psb: e.matmul(
                                out=psb[:], lhsT=self.onesf[:],
                                rhs=self.frow[:, ro * T + c * CH:ro * T + (c + 1) * CH],
                                start=True, stop=True),
                            inc="st", key=f"{w}bc{dt}_{c}")
                    self.wt("vector", f"{w}bc{dt}_{c}")
                    self.op("vector",
                            lambda e, psb=psb: e.tensor_copy(out=self.bbb[:],
                                                             in_=psb[:]))
                    self.op("vector",
                            lambda e, dt=dt, c=c, buf=buf: e.tensor_mul(
                                out=buf[:, dt * T + c * CH:dt * T + (c + 1) * CH],
                                in0=buf[:, dt * T + c * CH:dt * T + (c + 1) * CH],
                                in1=self.bbb[:]))
                    self.op("vector",
                            lambda e, dt=dt, c=c, buf=buf, go=go: e.tensor_scalar(
                                out=buf[:, dt * T + c * CH:dt * T + (c + 1) * CH],
                                in0=buf[:, dt * T + c * CH:dt * T + (c + 1) * CH],
                                scalar1=self.gcols[:, go + dt:go + dt + 1],
                                scalar2=None, op0=ALU.mult),
                            inc="sv", key=f"{w}sc{dt}_{c}")
        if self.astop <= 1:
            return
        for sM in range(32):
            self.wt("tensor", f"xnsc7_7")
            if sM >= 2:
                self.wt("tensor", f"ttke{sM-2}")
            bk = (sM % 2) * 1024
            for dt in range(8):
                self.op("tensor",
                        lambda e, sM=sM, dt=dt, bk=bk: e.transpose(
                            out=self.ps_t[:, bk + dt * 128:bk + (dt + 1) * 128],
                            in_=self.xnb[:, dt * T + sM * 128:dt * T + (sM + 1) * 128],
                            identity=self.idb[:]),
                        inc="st" if dt == 7 else None,
                        key=f"ttk{sM}" if dt == 7 else None)
            self.wt("vector", f"ttk{sM}")
            if sM >= 1:
                self.wt("vector", f"ttst{sM-1}")
            self.op("vector",
                    lambda e, bk=bk: e.tensor_copy(
                        out=self.tokc[:], in_=self.ps_t[:, bk:bk + 1024]),
                    inc="sv", key=f"ttke{sM}")
            self.wt("gpsimd", f"ttke{sM}")
            self.sdma("gpsimd", io["xrmstok"][sM * 128:(sM + 1) * 128, :],
                      self.tokc[:], q=f"Ft{sM % 2}", key=f"ttst{sM}")
        if self.astop <= 2:
            return
        for sM in range(32):
            for ec in range(8):
                n_ = sM * 8 + ec
                par = n_ % 2
                self.wt("tensor", "sesc7_7")
                if n_ >= 2:
                    kk = n_ - 2
                    self.wt("tensor", f"aev{kk//8}_{kk%8}")
                for dt in range(8):
                    self.op("tensor",
                            lambda e, sM=sM, ec=ec, dt=dt, par=par: e.matmul(
                                out=self.ps_a[:, par * CH:(par + 1) * CH],
                                lhsT=self.xnb[:, dt * T + sM * 128:dt * T + (sM + 1) * 128],
                                rhs=self.seb[:, dt * T + ec * CH:dt * T + (ec + 1) * CH],
                                start=(dt == 0), stop=(dt == 7)),
                            inc="st" if dt == 7 else None,
                            key=f"amm{sM}_{ec}" if dt == 7 else None)
                self.wt("scalar", f"amm{sM}_{ec}")
                if n_ >= 2:
                    kk = n_ - 2
                    self.wt("scalar", f"atp{kk//8}_{kk%8}")
                    self.wt("scalar", f"ast{kk//8}_{kk%8}")
                    self.wt("scalar", f"ate{kk//8}_{kk%8}")
                self.op("scalar",
                        lambda e, par=par: e.activation(
                            out=self.At[:, par * CH:(par + 1) * CH],
                            in_=self.ps_a[:, par * CH:(par + 1) * CH],
                            func=AF.Copy),
                        inc="ss", key=f"aev{sM}_{ec}")
                self.wt("vector", f"aev{sM}_{ec}")
                self.op("vector",
                        lambda e, par=par: e.reduce_max(
                            out=self.mtmp[:], in_=self.At[:, par * CH:(par + 1) * CH],
                            axis=AX.X))
                self.op("vector",
                        lambda e, sM=sM: e.tensor_max(
                            out=self.mcg[:, sM:sM + 1], in0=self.mcg[:, sM:sM + 1],
                            in1=self.mtmp[:]))
                self.op("vector",
                        lambda e, ec=ec, par=par: e.tensor_max(
                            out=self.mdacc[:, ec * CH:(ec + 1) * CH],
                            in0=self.mdacc[:, ec * CH:(ec + 1) * CH],
                            in1=self.At[:, par * CH:(par + 1) * CH]),
                        inc="sv", key=f"amx{sM}_{ec}")
                self.wt("gpsimd", f"aev{sM}_{ec}")
                self.sdma("gpsimd",
                          io["Adram"][sM * 128:(sM + 1) * 128, ec * CH:(ec + 1) * CH],
                          self.At[:, par * CH:(par + 1) * CH], q=f"Fa{par}",
                          key=f"ast{sM}_{ec}")
                self.wt("tensor", f"aev{sM}_{ec}")
                if n_ < 2:
                    self.wt("tensor", f"ttke{31}")
                if n_ >= 2:
                    kk = n_ - 2
                    self.wt("tensor", f"ate{kk//8}_{kk%8}")
                for tt in range(4):
                    self.op("tensor",
                            lambda e, tt=tt, par=par: e.transpose(
                                out=self.ps_t[:, par * 1024 + tt * 128:
                                              par * 1024 + (tt + 1) * 128],
                                in_=self.At[:, par * CH + tt * 128:
                                            par * CH + (tt + 1) * 128],
                                identity=self.idb[:]),
                            inc="st" if tt == 3 else None,
                            key=f"atp{sM}_{ec}" if tt == 3 else None)
                self.wt("vector", f"atp{sM}_{ec}")
                if n_ >= 2:
                    kk = n_ - 2
                    self.wt("vector", f"atst{kk//8}_{kk%8}")
                self.op("vector",
                        lambda e, par=par: e.tensor_copy(
                            out=self.ATt[:, par * CH:(par + 1) * CH],
                            in_=self.ps_t[:, par * 1024:par * 1024 + CH]),
                        inc="sv", key=f"ate{sM}_{ec}")
                self.wt("gpsimd", f"ate{sM}_{ec}")
                for tt in range(4):
                    self.sdma("gpsimd",
                              io["ATdram"][ec * CH + tt * 128:ec * CH + (tt + 1) * 128,
                                           sM * 128:(sM + 1) * 128],
                              self.ATt[:, par * CH + tt * 128:par * CH + (tt + 1) * 128],
                              q=f"Fb{par}",
                              key=f"atst{sM}_{ec}" if tt == 3 else None)
        if self.astop <= 3:
            return
        for j in range(32):
            self.wt("tensor", f"amx{31}_{7}")
            if j < 2:
                self.wt("tensor", f"ate{31}_{7}")
            if j >= 2:
                self.wt("tensor", f"mde{j-2}")
            self.op("tensor",
                    lambda e, j=j: e.transpose(
                        out=self.ps_t[0:128, (j % 2) * 1024:(j % 2) * 1024 + 128],
                        in_=self.mdacc[:, j * 128:(j + 1) * 128],
                        identity=self.idb[:]),
                    inc="st", key=f"mdt{j}")
            self.wt("vector", f"mdt{j}")
            self.op("vector",
                    lambda e, j=j: e.reduce_max(
                        out=self.mdg[:, j:j + 1],
                        in_=self.ps_t[0:128, (j % 2) * 1024:(j % 2) * 1024 + 128],
                        axis=AX.X),
                    inc="sv", key=f"mde{j}")
        if self.astop <= 4:
            return
        for w, grid in (("mc", self.mcg), ("md", self.mdg)):
            self.wt("tensor", f"mde{31}" if w == "mc" else "mcge")
            self.op("tensor",
                    lambda e, grid=grid: e.transpose(
                        out=self.ps_r[0:32, 0:128], in_=grid[:, 0:32],
                        identity=self.idf2[:]),
                    inc="st", key=f"{w}gt")
            self.wt("vector", f"{w}gt")
            self.op("vector",
                    lambda e: e.tensor_copy(out=self.mgt[:], in_=self.ps_r[0:32, 0:128]),
                    inc="sv", key=f"{w}ge")
            self.wt("gpsimd", f"{w}ge")
            self.sdma("gpsimd", io[f"{w}row"].rearrange("(a b) -> a b", b=128),
                      self.mgt[:], q="Fr", key=f"{w}st")
            self.wt("vector", f"{w}st")


class P_F1B(SPhase):
    def alloc(self):
        sb, ps = self.sb, self.psb
        t = "fb"
        self.slT = sb(t + "slT", [128, 8 * T], BF16)
        self.idb = sb(t + "idb", [128, 128], BF16)
        self.onesb = sb(t + "onesb", [128, 1], BF16)
        self.onesf = sb(t + "onesf", [1, 128])
        self.onesfb = sb(t + "onesfb", [1, 128], BF16)
        self.zr = sb(t + "zr", [1, CH])
        self.zc = sb(t + "zc", [128, 4])
        self.w2st = [sb(t + f"w2st{i}", [128, CH]) for i in range(2)]
        self.w2b = [sb(t + f"w2b{i}", [128, CH], BF16) for i in range(2)]
        self.prt = [sb(t + f"prt{i}", [128, CH], BF16) for i in range(2)]
        self.hTc = sb(t + "hTc", [128, 32 * CH], BF16)
        self.wstack = ExitStack()
        wsb = lambda n, sh, dt: self.wstack.enter_context(
            self.ctx.nc.sbuf_tensor(t + n, sh, dt))
        self.MD = wsb("MD", [128, T], BF16)
        self.mrow = wsb("mrow", [1, T], BF16)
        self.at = [wsb(f"at{i}", [128, CH], BF16) for i in range(2)]
        self.stokA = wsb("stokA", [128, 4 * CH], BF16)
        self.stokB = wsb("stokB", [128, 4 * CH], BF16)
        self.xtok = [wsb(f"xtok{i}", [128, 1024], BF16) for i in range(2)]
        self.dsp = wsb("dsp", [128, 32 * CH], BF16)
        self.wstack.close()
        self.w1b = sb(t + "w1b", [128, 8 * T], BF16)
        self.w1s = sb(t + "w1s", [128, T])
        self.ps_z = ps(t + "ps_z", [1, CH])
        self.ps_sl = ps(t + "ps_sl", [128, 4 * CH])
        self.ps_t = ps(t + "ps_t", [128, 4 * CH], BF16)

    def build(self):
        io = self.io
        self.op("vector", lambda e: e.memset(self.onesb[:], 1.0))
        self.op("vector", lambda e: e.memset(self.onesfb[:], 1.0))
        self.op("vector", lambda e: e.memset(self.onesf[:], 1.0), inc="sv",
                key="binit")
        self.sdma("sync", self.idb[:], io["identb"], q="Bc")
        self.sdma("sync", self.mrow[:], io["mdrow"].rearrange("(o x) -> o x", o=1), q="Bc",
                  key="bmd")
        for c in range(8):
            self.wt("tensor", "bmd")
            if c >= 2:
                self.wt("tensor", f"bmde{c-2}")
            self.wt("tensor", "binit")
            self.op("tensor",
                    lambda e, c=c: e.matmul(
                        out=self.ps_sl[:, (c % 2) * CH:(c % 2 + 1) * CH],
                        lhsT=self.onesfb[:], rhs=self.mrow[:, c * CH:(c + 1) * CH],
                        start=True, stop=True),
                    inc="st", key=f"bmdb{c}")
            self.wt("vector", f"bmdb{c}")
            self.op("vector",
                    lambda e, c=c: e.tensor_copy(
                        out=self.MD[:, c * CH:(c + 1) * CH],
                        in_=self.ps_sl[:, (c % 2) * CH:(c % 2 + 1) * CH]),
                    inc="sv", key=f"bmde{c}")
        if "dbgMD" in io:
            self.wt("gpsimd", f"bmde{7}")
            self.sdma("gpsimd", io["dbgMD"], self.MD[:], q="Dbg", key="dMD")
        for ec in range(8):
            for sM in range(32):
                n_ = ec * 32 + sM
                par = sM % 2
                if n_ >= 2:
                    kk = n_ - 2
                    self.wt("sync", f"bds{kk//32}_{kk%32}")
                self.sdma("sync", self.at[par][:],
                          io["Adram"][sM * 128:(sM + 1) * 128, ec * CH:(ec + 1) * CH],
                          q=f"Ba{par}", key=f"bal{ec}_{sM}")
                self.wt("vector", f"bal{ec}_{sM}")
                self.wt("vector", f"bmde{7}")
                if ec >= 1 and sM == 0:
                    self.wt("vector", f"bsl{ec-1}_3")
                self.op("vector",
                        lambda e, par=par, ec=ec: e.tensor_sub(
                            out=self.at[par][:], in0=self.at[par][:],
                            in1=self.MD[:, ec * CH:(ec + 1) * CH]),
                        inc="sv", key=f"bsub{ec}_{sM}")
                self.wt("scalar", f"bsub{ec}_{sM}")
                if ec >= 1 and sM == 0:
                    self.wt("scalar", f"bsm{ec-1}_1")
                    if ec == 1 and "dbgdsp" in io:
                        self.wt("scalar", "ddsp")
                self.op("scalar",
                        lambda e, par=par, sM=sM: e.activation(
                            out=self.dsp[:, sM * CH:(sM + 1) * CH],
                            in_=self.at[par][:], func=AF.Exp),
                        inc="ss", key=f"bds{ec}_{sM}")
                self.wt("tensor", f"bds{ec}_{sM}")
                if sM == 0 and ec >= 1:
                    self.wt("tensor", f"bz{ec-1}")
                self.op("tensor",
                        lambda e, sM=sM: e.matmul(
                            out=self.ps_z[:], lhsT=self.onesb[:],
                            rhs=self.dsp[:, sM * CH:(sM + 1) * CH],
                            start=(sM == 0), stop=(sM == 31)),
                        inc="st" if sM == 31 else None,
                        key=f"bzm{ec}" if sM == 31 else None)
            if ec == 0 and "dbgdsp" in io:
                self.wt("gpsimd", f"bds{0}_{31}")
                self.sdma("gpsimd", io["dbgdsp"], self.dsp[:], q="Dbg", key="ddsp")
            self.wt("vector", f"bzm{ec}")
            self.op("vector",
                    lambda e: e.tensor_copy(out=self.zr[:], in_=self.ps_z[:]),
                    inc="sv", key=f"bz{ec}")
            for etM in range(4):
                self.wt("tensor", f"bz{ec}")
                if ec >= 1 and etM == 0:
                    self.wt("tensor", f"bse{ec-1}_{1}")
                self.op("tensor",
                        lambda e, etM=etM: e.matmul(
                            out=self.ps_sl[0:128, etM * CH:etM * CH + 1],
                            lhsT=self.zr[:, etM * 128:(etM + 1) * 128],
                            rhs=self.onesf[:, 0:1], start=True, stop=True),
                        inc="st" if etM == 3 else None,
                        key=f"bzt{ec}" if etM == 3 else None)
            self.wt("vector", f"bzt{ec}")
            if ec >= 1 and "dbgzc" in io:
                self.wt("vector", f"dzc{ec-1}")
            self.op("vector",
                    lambda e: e.tensor_copy(
                        out=self.zc[:], in_=self.ps_sl[:, 0:4 * CH].rearrange(
                            "p (a b) -> p (b a)", b=CH)[:, 0:4]) if False else
                    e.tensor_copy(out=self.zc[:, 0:1], in_=self.ps_sl[:, 0:1]))
            self.op("vector", lambda e: e.tensor_copy(out=self.zc[:, 1:2],
                                                      in_=self.ps_sl[:, CH:CH + 1]))
            self.op("vector", lambda e: e.tensor_copy(out=self.zc[:, 2:3],
                                                      in_=self.ps_sl[:, 2 * CH:2 * CH + 1]))
            self.op("vector", lambda e: e.tensor_copy(out=self.zc[:, 3:4],
                                                      in_=self.ps_sl[:, 3 * CH:3 * CH + 1]),
                    inc="sv", key=f"bzc{ec}")
            self.wt("vector", f"bzc{ec}")
            self.op("vector", lambda e: e.reciprocal(out=self.zc[:], in_=self.zc[:]),
                    inc="sv", key=f"bzr{ec}")
            if "dbgzc" in io:
                self.wt("gpsimd", f"bzr{ec}")
                self.sdma("gpsimd", io["dbgzc"][ec], self.zc[:], q="Dbg",
                          key=f"dzc{ec}")
            for dh in range(2):
                for sM in range(32):
                    n4 = (ec * 2 + dh) * 32 + sM
                    if n4 >= 2:
                        kk = n4 - 2
                        self.wt("sync", f"bsms{kk//32}_{kk%32}")
                    self.sdma("sync", self.xtok[sM % 2][:],
                              io["xrmstok"][sM * 128:(sM + 1) * 128, :],
                              q=f"Bx{sM % 2}", key=f"bxt{ec}_{dh}_{sM}")
                    self.wt("tensor", f"bxt{ec}_{dh}_{sM}")
                    if sM == 0:
                        self.wt("tensor", f"bzr{ec}")
                        if ec * 2 + dh >= 1:
                            pk = ec * 2 + dh - 1
                            self.wt("tensor", f"bse{pk//2}_{pk%2}")
                    for etM in range(4):
                        self.op("tensor",
                                lambda e, sM=sM, etM=etM, dh=dh: e.matmul(
                                    out=self.ps_sl[:, etM * CH:(etM + 1) * CH],
                                    lhsT=self.dsp[:, sM * CH + etM * 128:
                                                  sM * CH + (etM + 1) * 128],
                                    rhs=self.xtok[sM % 2][:, dh * 512:(dh + 1) * 512],
                                    start=(sM == 0), stop=(sM == 31)),
                                inc="st" if etM == 3 else None,
                                key=(f"bsm{ec}_{dh}" if (sM == 31 and etM == 3)
                                     else (f"bsms{(ec*2+dh)//2 if False else 0}_{0}"
                                           if False else
                                           (f"bsms{ec*2+dh}_{sM}" if etM == 3 else None))))
                for etM in range(4):
                    self.wt("vector", f"bsm{ec}_{dh}")
                    self.wt("vector", f"bzr{ec}")
                    dst = self.stokA if dh == 0 else self.stokB
                    self.op("vector",
                            lambda e, etM=etM, dst=dst: e.tensor_scalar(
                                out=dst[:, etM * CH:(etM + 1) * CH],
                                in0=self.ps_sl[:, etM * CH:(etM + 1) * CH],
                                scalar1=self.zc[:, etM:etM + 1], scalar2=None,
                                op0=ALU.mult),
                            inc="sv" if etM == 3 else None,
                            key=f"bse{ec}_{dh}" if etM == 3 else None)
            for etM in range(4):
                self.wt("tensor", f"bse{ec}_{1}")
                n_ = ec * 4 + etM
                if n_ >= 2:
                    kk = n_ - 2
                    self.wt("tensor", f"bsl{kk//4}_{kk%4}")
                bk = (etM % 2) * 1024
                for dt in range(8):
                    src = self.stokA if dt < 4 else self.stokB
                    self.op("tensor",
                            lambda e, dt=dt, etM=etM, src=src, bk=bk: e.transpose(
                                out=self.ps_t[:, bk + dt * 128:bk + (dt + 1) * 128],
                                in_=src[:, etM * CH + (dt % 4) * 128:
                                        etM * CH + (dt % 4 + 1) * 128],
                                identity=self.idb[:]),
                            inc="st" if dt == 7 else None,
                            key=f"btp{ec}_{etM}" if dt == 7 else None)
                self.wt("vector", f"btp{ec}_{etM}")
                for dt in range(8):
                    self.op("vector",
                            lambda e, dt=dt, ec=ec, etM=etM, bk=bk:
                            e.tensor_copy(
                                out=self.slT[:, dt * T + ec * CH + etM * 128:
                                             dt * T + ec * CH + (etM + 1) * 128],
                                in_=self.ps_t[:, bk + dt * 128:bk + (dt + 1) * 128]),
                            inc="sv" if dt == 7 else None,
                            key=f"bsl{ec}_{etM}" if dt == 7 else None)
            self.alias(f"bsms{ec*2}_{31}", f"bsm{ec}_{0}")
            self.alias(f"bsms{ec*2+1}_{31}", f"bsm{ec}_{1}")
        if "dbgslT" in io:
            self.wt("gpsimd", f"bsl{7}_{3}")
            self.sdma("gpsimd", io["dbgslT"], self.slT[:], q="Dbg", key="dslT")
        for dt in range(8):
            if dt == 0:
                self.wt("sync", f"bsl{7}_{3}")
            if dt >= 1:
                self.wt("sync", f"bw1c{dt-1}")
            self.sdma("sync", self.w1s[:], io["w1f"][dt], q=f"Bw{dt % 2}",
                      key=f"bw1{dt}")
            self.wt("vector", f"bw1{dt}")
            self.op("vector",
                    lambda e, dt=dt: e.tensor_copy(
                        out=self.w1b[:, dt * T:(dt + 1) * T], in_=self.w1s[:]),
                    inc="sv", key=f"bw1c{dt}")
        for ec in range(8):
            for hM in range(32):
                n_ = ec * 32 + hM
                par = hM % 2
                self.wt("tensor", f"bw1c{7}")
                if hM == 0 and ec >= 1:
                    self.wt("tensor", f"bpe{ec-1}_{1}")
                if n_ >= 2:
                    kk = n_ - 2
                    self.wt("tensor", f"bge{kk//32}_{kk%32}")
                for dt in range(8):
                    self.op("tensor",
                            lambda e, ec=ec, hM=hM, dt=dt, par=par: e.matmul(
                                out=self.ps_sl[:, par * CH:(par + 1) * CH],
                                lhsT=self.w1b[:, dt * T + hM * 128:dt * T + (hM + 1) * 128],
                                rhs=self.slT[:, dt * T + ec * CH:dt * T + (ec + 1) * CH],
                                start=(dt == 0), stop=(dt == 7)),
                            inc="st" if dt == 7 else None,
                            key=f"bgm{ec}_{hM}" if dt == 7 else None)
                self.wt("scalar", f"bgm{ec}_{hM}")
                self.op("scalar",
                        lambda e, hM=hM, par=par: e.activation(
                            out=self.hTc[:, hM * CH:(hM + 1) * CH],
                            in_=self.ps_sl[:, par * CH:(par + 1) * CH],
                            func=AF.Gelu),
                        inc="ss", key=f"bge{ec}_{hM}")
            for p in range(2):
                for hid in range(32):
                    jj = (ec * 2 + p) * 32 + hid
                    if jj >= 2:
                        kk = jj - 2
                        self.wt("sync", f"bw2c{kk}")
                    self.sdma("sync", self.w2st[hid % 2][:],
                              io["w2f"][hid].rearrange("a b -> a b")[:, p * CH:(p + 1) * CH],
                              q=f"Bv{hid % 2}", key=f"bw2{jj}")
                    self.wt("scalar", f"bw2{jj}")
                    if jj >= 2:
                        self.wt("scalar", f"bpm{jj-2}")
                    self.op("scalar",
                            lambda e, hid=hid: e.activation(
                                out=self.w2b[hid % 2][:], in_=self.w2st[hid % 2][:],
                                func=AF.Copy),
                            inc="ss", key=f"bw2c{jj}")
                    self.wt("tensor", f"bw2c{jj}")
                    self.wt("tensor", f"bge{ec}_31")
                    if hid == 0 and (ec * 2 + p) >= 1:
                        pk = ec * 2 + p - 1
                        self.wt("tensor", f"bpe{pk//2}_{pk%2}")
                    for etM in range(4):
                        self.op("tensor",
                                lambda e, hid=hid, etM=etM: e.matmul(
                                    out=self.ps_sl[:, etM * CH:(etM + 1) * CH],
                                    lhsT=self.hTc[:, hid * CH + etM * 128:
                                                  hid * CH + (etM + 1) * 128],
                                    rhs=self.w2b[hid % 2][:],
                                    start=(hid == 0), stop=(hid == 31)),
                                inc="st" if etM == 3 else None,
                                key=f"bpm{jj}" if etM == 3 else None)
                for etM in range(4):
                    self.wt("vector", f"bpm{(ec*2+p)*32+31}")
                    n2 = (ec * 2 + p) * 4 + etM
                    if n2 >= 2:
                        kk = n2 - 2
                        self.wt("vector", f"bps{kk}")
                    self.op("vector",
                            lambda e, etM=etM: e.tensor_copy(
                                out=self.prt[etM % 2][:],
                                in_=self.ps_sl[:, etM * CH:(etM + 1) * CH]),
                            inc="sv", key=f"bpe_{n2}" if False else f"bpv{n2}")
                    self.wt("gpsimd", f"bpv{n2}")
                    self.sdma("gpsimd",
                              io["proc"][ec * CH + etM * 128:ec * CH + (etM + 1) * 128,
                                         p * CH:(p + 1) * CH],
                              self.prt[etM % 2][:], q=f"Bp{etM % 2}", key=f"bps{n2}")
                self.alias(f"bpe{ec}_{p}", f"bpv{(ec*2+p)*4+3}")


class P_F1C(SPhase):
    def alloc(self):
        sb, ps = self.sb, self.psb
        t = "fc"
        self.MC = sb(t + "MC", [128, T], BF16)
        self.mrow = sb(t + "mrow", [1, T], BF16)
        self.onesfb = sb(t + "onesfb", [1, 128], BF16)
        self.ptok = sb(t + "ptok", [128, 32 * 1024], BF16)
        self.cbuf = sb(t + "cbuf", [128, 32 * CH], BF16)
        self.at = [sb(t + f"at{i}", [128, CH], BF16) for i in range(2)]
        self.onesb = sb(t + "onesb", [128, 1], BF16)
        self.onesf = sb(t + "onesf", [1, 128])
        self.zr = sb(t + "zr", [1, CH])
        self.zc = sb(t + "zc", [128, 4])
        self.g2 = sb(t + "g2", [128, CH])
        self.x2t = [sb(t + f"x2t{i}", [128, CH]) for i in range(2)]
        self.s2t = [sb(t + f"s2t{i}", [128, CH]) for i in range(2)]
        self.ob = [sb(t + f"ob{i}", [128, CH]) for i in range(2)]
        self.ps_z = ps(t + "ps_z", [1, CH])
        self.ps_o = ps(t + "ps_o", [128, 2 * CH])
        self.ps_c = ps(t + "ps_c", [128, CH])

    def build(self):
        io = self.io
        self.op("vector", lambda e: e.memset(self.onesb[:], 1.0))
        self.op("vector", lambda e: e.memset(self.onesfb[:], 1.0))
        self.op("vector", lambda e: e.memset(self.onesf[:], 1.0), inc="sv",
                key="cinit2")
        self.sdma("sync", self.mrow[:], io["mcrow"].rearrange("(o x) -> o x", o=1), q="Gc",
                  key="gmc")
        for sM in range(32):
            self.sdma("sync", self.ptok[:, sM * 1024:(sM + 1) * 1024],
                      io["proc"][sM * 128:(sM + 1) * 128, :], q=f"Gp{sM % 2}",
                      key=f"gpt{sM}")
        for c in range(8):
            self.wt("tensor", "gmc")
            self.wt("tensor", "cinit2")
            if c >= 2:
                self.wt("tensor", f"gmde{c-2}")
            self.op("tensor",
                    lambda e, c=c: e.matmul(
                        out=self.ps_o[:, (c % 2) * CH:(c % 2 + 1) * CH],
                        lhsT=self.onesfb[:], rhs=self.mrow[:, c * CH:(c + 1) * CH],
                        start=True, stop=True),
                    inc="st", key=f"gmdb{c}")
            self.wt("vector", f"gmdb{c}")
            self.wt("vector", "cinit2")
            self.op("vector",
                    lambda e, c=c: e.tensor_copy(
                        out=self.MC[:, c * CH:(c + 1) * CH],
                        in_=self.ps_o[:, (c % 2) * CH:(c % 2 + 1) * CH]),
                    inc="sv", key=f"gmde{c}")
        for sc in range(8):
            for et in range(32):
                n_ = sc * 32 + et
                par = et % 2
                if n_ >= 2:
                    kk = n_ - 2
                    self.wt("sync", f"gds{kk//32}_{kk%32}")
                self.sdma("sync", self.at[par][:],
                          io["ATdram"][et * 128:(et + 1) * 128, sc * CH:(sc + 1) * CH],
                          q=f"Ga{par}", key=f"gal{sc}_{et}")
                self.wt("vector", f"gal{sc}_{et}")
                self.wt("vector", f"gmde{7}")
                self.op("vector",
                        lambda e, par=par, sc=sc: e.tensor_sub(
                            out=self.at[par][:], in0=self.at[par][:],
                            in1=self.MC[:, sc * CH:(sc + 1) * CH]),
                        inc="sv", key=f"gsub{sc}_{et}")
                self.wt("scalar", f"gsub{sc}_{et}")
                if sc >= 1 and et == 0:
                    self.wt("scalar", f"gom{sc-1}_31")
                self.op("scalar",
                        lambda e, par=par, et=et: e.activation(
                            out=self.cbuf[:, et * CH:(et + 1) * CH],
                            in_=self.at[par][:], func=AF.Exp),
                        inc="ss", key=f"gds{sc}_{et}")
                self.wt("tensor", f"gds{sc}_{et}")
                if et == 0 and sc >= 1:
                    self.wt("tensor", f"gz{sc-1}")
                self.op("tensor",
                        lambda e, et=et: e.matmul(
                            out=self.ps_z[:], lhsT=self.onesb[:],
                            rhs=self.cbuf[:, et * CH:(et + 1) * CH],
                            start=(et == 0), stop=(et == 31)),
                        inc="st" if et == 31 else None,
                        key=f"gzm{sc}" if et == 31 else None)
            self.wt("vector", f"gzm{sc}")
            self.op("vector",
                    lambda e: e.tensor_copy(out=self.zr[:], in_=self.ps_z[:]),
                    inc="sv", key=f"gz{sc}")
            for sMq in range(4):
                self.wt("tensor", f"gz{sc}")
                if sc >= 1 and sMq == 0:
                    self.wt("tensor", f"gzc{sc-1}")
                self.op("tensor",
                        lambda e, sMq=sMq: e.matmul(
                            out=self.ps_c[0:128, sMq:sMq + 1],
                            lhsT=self.zr[:, sMq * 128:(sMq + 1) * 128],
                            rhs=self.onesf[:, 0:1], start=True, stop=True),
                        inc="st" if sMq == 3 else None,
                        key=f"gzt{sc}" if sMq == 3 else None)
            self.wt("vector", f"gzt{sc}")
            for sMq in range(4):
                self.op("vector",
                        lambda e, sMq=sMq: e.tensor_copy(
                            out=self.zc[:, sMq:sMq + 1],
                            in_=self.ps_c[:, sMq:sMq + 1]),
                        inc="sv" if sMq == 3 else None,
                        key=f"gzc{sc}" if sMq == 3 else None)
            self.wt("vector", f"gzc{sc}")
            self.op("vector", lambda e: e.reciprocal(out=self.zc[:], in_=self.zc[:]),
                    inc="sv", key=f"gzr{sc}")
            for sMq in range(4):
                sM = sc * 4 + sMq
                for et in range(32):
                    if et == 0:
                        self.wt("tensor", f"gzr{sc}")
                        self.wt("tensor", f"gpt{31}")
                        if sc * 4 + sMq >= 1:
                            n2 = sc * 4 + sMq - 1
                            self.wt("tensor", f"gev{n2//4}_{n2%4}")
                    for dh in range(2):
                        self.op("tensor",
                                lambda e, sMq=sMq, et=et, dh=dh: e.matmul(
                                    out=self.ps_o[:, dh * CH:(dh + 1) * CH],
                                    lhsT=self.cbuf[:, et * CH + sMq * 128:
                                                   et * CH + (sMq + 1) * 128],
                                    rhs=self.ptok[:, et * 1024 + dh * 512:
                                                  et * 1024 + (dh + 1) * 512],
                                    start=(et == 0), stop=(et == 31)),
                                inc="st" if (et == 31 and dh == 1) else None,
                                key=f"gom{sc}_{31}" if (et == 31 and dh == 1 and sMq == 3)
                                else (f"gomm{sc}_{sMq}" if (et == 31 and dh == 1) else None))
                omkey = f"gom{sc}_{31}" if sMq == 3 else f"gomm{sc}_{sMq}"
                n3 = sc * 4 + sMq
                for dh in range(2):
                    if n3 >= 1:
                        kk = n3 - 1
                        self.wt("sync", f"gst{kk}_{dh}")
                    self.sdma("sync", self.x2t[dh][:],
                              io["x2tok"][sM * 128:(sM + 1) * 128, dh * CH:(dh + 1) * CH],
                              q=f"Gx{dh}", key=f"gx{sM}_{dh}")
                    self.sdma("sync", self.s2t[dh][:],
                              io["s2tok"][sM * 128:(sM + 1) * 128, dh * CH:(dh + 1) * CH],
                              q=f"Gx{dh}", key=f"gs{sM}_{dh}")
                    self.wt("vector", omkey)
                    if n3 + dh >= 1:
                        self.wt("vector", f"gsi{sM if dh == 1 else sM - 1}_{1 - dh}")
                    self.op("vector",
                            lambda e, sMq=sMq, dh=dh: e.tensor_scalar(
                                out=self.g2[:],
                                in0=self.ps_o[:, dh * CH:(dh + 1) * CH],
                                scalar1=self.zc[:, sMq:sMq + 1], scalar2=None,
                                op0=ALU.mult),
                            inc="sv", key=f"gn{sM}_{dh}")
                    self.wt("scalar", f"gn{sM}_{dh}")
                    self.op("scalar",
                            lambda e: e.activation(out=self.g2[:], in_=self.g2[:],
                                                   func=AF.Silu),
                            inc="ss", key=f"gsi{sM}_{dh}")
                    self.wt("vector", f"gsi{sM}_{dh}")
                    self.wt("vector", f"gx{sM}_{dh}")
                    self.wt("vector", f"gs{sM}_{dh}")
                    self.op("vector",
                            lambda e, dh=dh: e.tensor_mul(
                                out=self.g2[:], in0=self.g2[:], in1=self.s2t[dh][:]))
                    self.op("vector",
                            lambda e, dh=dh: e.tensor_add(
                                out=self.ob[dh][:], in0=self.g2[:],
                                in1=self.x2t[dh][:]),
                            inc="sv", key=f"go{sM}_{dh}")
                    self.wt("gpsimd", f"go{sM}_{dh}")
                    self.sdma("gpsimd",
                              io["OUT"][sM * 128:(sM + 1) * 128, dh * CH:(dh + 1) * CH],
                              self.ob[dh][:], q=f"Go{dh}", key=f"gst{n3}_{dh}")
                self.alias(f"gev{sc}_{sMq}", f"gn{sM}_{1}")


def _prep_f1(inputs):
    seT = np.ascontiguousarray(
        np.asarray(inputs["L1_ffn_slot_embeds"], np.float32).reshape(4096, 1024).T)
    gxc = np.ascontiguousarray(
        np.asarray(inputs["L1_ffn_gamma_x"], np.float32).reshape(8, 128).T)
    gsc = np.ascontiguousarray(
        np.asarray(inputs["L1_ffn_gamma_slot"], np.float32).reshape(8, 128).T)
    w1f = np.ascontiguousarray(
        np.asarray(inputs["L1_ffn_w1"], np.float32).reshape(8, 128, 4096))
    w2f = np.ascontiguousarray(
        np.asarray(inputs["L1_ffn_w2"], np.float32).reshape(32, 128, 1024))
    return seT, gxc, gsc, w1f, w2f


E_IN = "ExternalInput"
E_OUT = "ExternalOutput"
DEBUG_SINK = None


def kernel(**inputs):
    x = np.asarray(inputs["L0_attn_x"], np.float32)
    B = x.shape[0]
    rg = _prep_rg_weights(inputs)
    rtb, ebb, identb = _blockdiags()
    identf = np.eye(128, dtype=np.float32)
    tabs = _l0_tables()
    masks = _l0_masks()
    a0 = {l: _prep_att0(inputs, l) for l in (0, 1)}
    m0 = {l: _prep_mlp(inputs, l) for l in (0, 1)}
    wq16, wkup, wvup, wot16, kdt = _prep_l1(inputs)
    seT, gxc, gsc, w1f, w2f = _prep_f1(inputs)
    cores = list(range(B))

    FT, BT = (D, T), (T, D)

    def run(phases, specs, in_map, tag=""):
        print(f"[mk] building {tag}", file=sys.stderr, flush=True)
        npdt = {k: mybir.dt.np(v[1]) for k, v in specs.items()}
        in_map = [{k: np.asarray(v, dtype=npdt[k]) for k, v in m.items()}
                  for m in in_map]
        if DEBUG_SINK is not None:
            print(f"[mk] {tag} in dtypes: " +
                  ", ".join(f"{k}:{v.dtype}" for k, v in in_map[0].items()),
                  file=sys.stderr, flush=True)
        nc, _ = build_program(phases, specs)
        print(f"[mk] launching {tag} n={nc.n_instructions()}", file=sys.stderr,
              flush=True)
        r = run_bass_kernel_spmd(nc, in_map, core_ids=cores)
        print(f"[mk] done {tag}", file=sys.stderr, flush=True)
        if DEBUG_SINK is not None:
            DEBUG_SINK.append((tag, r.results))
        return r.results

    r1 = run([lambda c, io: P_LN(c, io["x_inT"], io["xnf"], io["xnb"], "ln0")],
             {"x_inT": (FT, F32, E_IN), "xnf": (FT, F32, E_OUT),
              "xnb": (FT, BF16, E_OUT)},
             [{"x_inT": np.ascontiguousarray(x[b].T)} for b in range(B)], tag="L1")
    r2 = run([lambda c, io: P_RGLRU(c, io["xnf"], io["s1"],
                                    {k: io[k] for k in ("igw", "agw", "igb", "agb", "ap")})],
             {"xnf": (FT, F32, E_IN), "igw": ((8, 128, 128), F32, E_IN),
              "agw": ((8, 128, 128), F32, E_IN), "igb": ((128, 8), F32, E_IN),
              "agb": ((128, 8), F32, E_IN), "ap": ((128, 8), F32, E_IN),
              "s1": (FT, F32, E_OUT)},
             [dict(rg, xnf=r1[b]["xnf"]) for b in range(B)], tag="L2")

    cur_f = [r1[b]["xnf"] for b in range(B)]
    cur_b = [r1[b]["xnb"] for b in range(B)]
    for l in (0, 1):
        wqkv, wo, qk = a0[l]
        ra = run([lambda c, io, l=l: P_ATT0(c, io, l)],
                 {f"in{l}f": (FT, F32, E_IN), f"in{l}b": (FT, BF16, E_IN),
                  "identb": ((128, 128), BF16, E_IN),
                  "rtb": ((128, 128), BF16, E_IN), "ebb": ((128, 128), BF16, E_IN),
                  "masks": ((8, 128, 1024), F32, E_IN),
                  "tabs": ((4, 128, T), F32, E_IN),
                  f"wqkv{l}": ((8, 128, 1536), F32, E_IN),
                  f"wo{l}": ((4, 128, 1024), F32, E_IN),
                  f"qk{l}": ((128, 2), F32, E_IN),
                  f"mid{l}f": (FT, F32, E_OUT), f"mid{l}b": (FT, BF16, E_OUT)},
                 [{f"in{l}f": cur_f[b], f"in{l}b": cur_b[b], "identb": identb,
                   "rtb": rtb, "ebb": ebb, "masks": masks, "tabs": tabs,
                   f"wqkv{l}": wqkv, f"wo{l}": wo, f"qk{l}": qk}
                  for b in range(B)], tag="att0")
        w1, b1, w2h, b2, lng, lnb = m0[l]
        rm = run([lambda c, io, l=l: P_LN(c, io[f"mid{l}f"], None, io[f"xln{l}"],
                                          f"iln{l}", shift=True,
                                          gb=(io[f"lng_{l}"], io[f"lnb_{l}"])),
                  lambda c, io, l=l: P_MLP(c, io, l)],
                 {f"mid{l}f": (FT, F32, E_IN), "identb": ((128, 128), BF16, E_IN),
                  f"w1_{l}": ((8, 128, 5460), F32, E_IN),
                  f"b1_{l}": ((1, 5460), F32, E_IN),
                  f"w2h_{l}": ((2, 22, 128, 512), F32, E_IN),
                  f"b2_{l}": ((128, 8), F32, E_IN),
                  f"lng_{l}": ((128, 8), F32, E_IN),
                  f"lnb_{l}": ((128, 8), F32, E_IN),
                  f"xln{l}": (FT, BF16, "Internal"),
                  f"xf{l}f": (FT, F32, E_OUT), f"xf{l}b": (FT, BF16, E_OUT)},
                 [{f"mid{l}f": ra[b][f"mid{l}f"], "identb": identb,
                   f"w1_{l}": w1, f"b1_{l}": b1, f"w2h_{l}": w2h, f"b2_{l}": b2,
                   f"lng_{l}": lng, f"lnb_{l}": lnb} for b in range(B)], tag="mlp")
        cur_f = [rm[b][f"xf{l}f"] for b in range(B)]
        cur_b = [rm[b][f"xf{l}b"] for b in range(B)]

    r7 = run([lambda c, io: P_COMB(c, io)],
             {"x_inT": (FT, F32, E_IN), "s1": (FT, F32, E_IN),
              "g1": (FT, F32, E_IN), "identf": ((128, 128), F32, E_IN),
              "xn2b": (FT, BF16, E_OUT), "x2tok": (BT, F32, E_OUT)},
             [{"x_inT": np.ascontiguousarray(x[b].T), "s1": r2[b]["s1"],
               "g1": cur_f[b], "identf": identf} for b in range(B)], tag="L7")

    r8 = run([lambda c, io: P_L1W(c, io), lambda c, io: P_L1A(c, io)],
             {"xn2b": (FT, BF16, E_IN), "identb": ((128, 128), BF16, E_IN),
              "identf": ((128, 128), F32, E_IN),
              "wq16": ((16, 128, 1024), F32, E_IN),
              "wkup": ((16, 128, 64), F32, E_IN),
              "wvup": ((16, 128, 64), F32, E_IN),
              "wot16": ((16, 128, 1024), F32, E_IN),
              "kdt": ((8, 128, 64), F32, E_IN),
              "wqk_d": ((8, 128, 512), BF16, "Internal"),
              "wvo_d": ((4, 128, 1024), BF16, "Internal"),
              "kdb_d": ((128, 512), BF16, "Internal"),
              "zb": ((8, T), F32, "Internal"),
              "s2tok": (BT, F32, E_OUT)},
             [{"xn2b": r7[b]["xn2b"], "identb": identb, "identf": identf,
               "wq16": wq16, "wkup": wkup, "wvup": wvup, "wot16": wot16,
               "kdt": kdt} for b in range(B)], tag="L8")

    r9 = run([lambda c, io: P_F1A(c, io)],
             {"xn2b": (FT, BF16, E_IN), "seT": (FT, F32, E_IN),
              "identb": ((128, 128), BF16, E_IN),
              "identf": ((128, 128), F32, E_IN),
              "gxc": ((128, 8), F32, E_IN), "gsc": ((128, 8), F32, E_IN),
              "Adram": ((T, T), BF16, E_OUT), "ATdram": ((T, T), BF16, E_OUT),
              "xrmstok": (BT, BF16, E_OUT),
              "mcrow": ((T,), BF16, E_OUT), "mdrow": ((T,), BF16, E_OUT)},
             [{"xn2b": r7[b]["xn2b"], "seT": seT, "identb": identb,
               "identf": identf, "gxc": gxc, "gsc": gsc} for b in range(B)], tag="L9")

    dbg10 = {}
    if DEBUG_SINK is not None:
        dbg10 = {"dbgMD": ((128, T), BF16, E_OUT),
                 "dbgdsp": ((128, 32 * CH), BF16, E_OUT),
                 "dbgzc": ((8, 128, 4), F32, E_OUT),
                 "dbgslT": ((128, 8 * T), BF16, E_OUT)}
    r10 = run([lambda c, io: P_F1B(c, io)],
              {"Adram": ((T, T), BF16, E_IN), "xrmstok": (BT, BF16, E_IN),
               "mdrow": ((T,), BF16, E_IN), "identb": ((128, 128), BF16, E_IN),
               "w1f": ((8, 128, 4096), F32, E_IN),
               "w2f": ((32, 128, 1024), F32, E_IN),
               "proc": (BT, BF16, E_OUT), **dbg10},
              [{"Adram": r9[b]["Adram"], "xrmstok": r9[b]["xrmstok"],
                "mdrow": r9[b]["mdrow"], "identb": identb, "w1f": w1f,
                "w2f": w2f} for b in range(B)], tag="L10")

    r11 = run([lambda c, io: P_F1C(c, io)],
              {"ATdram": ((T, T), BF16, E_IN), "mcrow": ((T,), BF16, E_IN),
               "proc": (BT, BF16, E_IN), "x2tok": (BT, F32, E_IN),
               "s2tok": (BT, F32, E_IN), "OUT": (BT, F32, E_OUT)},
              [{"ATdram": r9[b]["ATdram"], "mcrow": r9[b]["mcrow"],
                "proc": r10[b]["proc"], "x2tok": r7[b]["x2tok"],
                "s2tok": r8[b]["s2tok"]} for b in range(B)], tag="L11")

    return np.stack([np.asarray(r11[b]["OUT"], np.float32) for b in range(B)])
